# version 03
# baseline (speedup 1.0000x reference)
import os
import numpy as np
import concourse.bass as bass
import concourse.mybir as mybir
from concourse.bass_utils import run_bass_kernel_spmd
from concourse.bass import IndirectOffsetOnAxis
from contextlib import ExitStack

F32 = mybir.dt.float32
BF16 = mybir.dt.bfloat16
I32 = mybir.dt.int32
ALU = mybir.AluOpType
ACT = mybir.ActivationFunctionType
AX = mybir.AxisListType.X

NCORE = 8
T = 4096
D = 1024
FC = 8
NCH = 8
CHT = 512
E = 8
FF = 4096
HC = 32
CAP = 5120
CPAD = 5248
NTILE = 41
STOT = 32768
TWO_PI = 6.283185307179586
EPS = 1e-5
MAGIC = float(2.0 ** 23)
MAGIC2 = float(1.5 * 2.0 ** 23)
PI = float(np.pi)


def _trunc_bits(x, bits):
    m, e = np.frexp(np.float64(x))
    return float(np.ldexp(np.floor(np.ldexp(m, bits)), int(e - bits)))


CW1 = _trunc_bits(TWO_PI, 11)
CW2 = _trunc_bits(TWO_PI - CW1, 11)
CW3 = float(np.float64(TWO_PI) - CW1 - CW2)

BLOCKS = [(b * 4, 4) for b in range(10)] + [(40, 1)]


def build(limit=4, debug=0):
    nc = bass.Bass("TRN2", target_bir_lowering=False, debug=False)

    xT_in = nc.dram_tensor("xT", [128, FC, T], F32, kind="ExternalInput").ap()
    wvT_in = nc.dram_tensor("wvT", [128, FC, D], F32, kind="ExternalInput").ap()
    woT_in = nc.dram_tensor("woT", [128, FC, D], F32, kind="ExternalInput").ap()
    gw_in = nc.dram_tensor("gw", [128, FC, E], F32, kind="ExternalInput").ap()
    gb_in = nc.dram_tensor("gb", [E, 1], F32, kind="ExternalInput").ap()
    fr_in = nc.dram_tensor("fr", [128, FC], F32, kind="ExternalInput").ap()
    cid_in = nc.dram_tensor("cid", [128, 1], F32, kind="ExternalInput").ap()
    psel_in = nc.dram_tensor("psel", [128, 16], F32, kind="ExternalInput").ap()
    w1_in = nc.dram_tensor("w1", [D, FF], F32, kind="ExternalInput").ap()
    b1_in = nc.dram_tensor("b1", [128, HC], F32, kind="ExternalInput").ap()
    w2_in = nc.dram_tensor("w2", [FF, D], F32, kind="ExternalInput").ap()
    b2_in = nc.dram_tensor("b2", [1, D], F32, kind="ExternalInput").ap()
    zin_in = nc.dram_tensor("zin", [512, D], F32, kind="ExternalInput").ap()
    out_t = nc.dram_tensor("yout", [T, D], F32, kind="ExternalOutput").ap()
    if debug:
        dbg_x1 = nc.dram_tensor("dbg_x1", [T, D], F32, kind="ExternalOutput").ap()
        dbg_h2 = nc.dram_tensor("dbg_h2", [T, D], F32, kind="ExternalOutput").ap()
        dbg_lg = nc.dram_tensor("dbg_lg", [STOT, E], F32, kind="ExternalOutput").ap()
        dbg_ls = nc.dram_tensor("dbg_ls", [CPAD + 2, 2], F32, kind="ExternalOutput").ap()
        dbg_om = nc.dram_tensor("dbg_om", [CPAD, D], F32, kind="ExternalOutput").ap()
        dbg_rt = nc.dram_tensor("dbg_rt", [128, 8, 256], F32, kind="ExternalOutput").ap()
        dbg_h2a = nc.dram_tensor("dbg_h2a", [8192, D], BF16, kind="ExternalOutput").ap()
        dbg_xg = nc.dram_tensor("dbg_xg", [128, 2, D], BF16, kind="ExternalOutput").ap()
        dbg_gi = nc.dram_tensor("dbg_gi", [128, NTILE], I32, kind="ExternalOutput").ap()
        dbg_xi = nc.dram_tensor("dbg_xi", [128, 8, 128], BF16, kind="ExternalOutput").ap()
        dbg_hb = nc.dram_tensor("dbg_hb", [128, 32, 128], BF16, kind="ExternalOutput").ap()

    x1sp = nc.dram_tensor("x1sp", [T, D], F32).ap()
    h2m = nc.dram_tensor("h2m", [T, D], BF16).ap()
    h2all = nc.dram_tensor("h2all", [STOT, D], BF16).ap()
    lgm = nc.dram_tensor("lgm", [T, E], F32).ap()
    lgall = nc.dram_tensor("lgall", [STOT, E], F32).ap()
    w2b = nc.dram_tensor("w2b", [FF, D], BF16).ap()
    w1b = nc.dram_tensor("w1b", [HC, 128, FC, 128], BF16).ap()
    listd = nc.dram_tensor("listd", [CPAD + 2, 2], F32).ap()
    bar_in = nc.dram_tensor("bar_in", [8, 4], F32).ap()
    bar_out = nc.dram_tensor("bar_out", [8, 4], F32).ap()
    bar2_out = nc.dram_tensor("bar2_out", [8, 4], F32).ap()
    bar3_out = nc.dram_tensor("bar3_out", [8, 4], F32).ap()
    bar4_out = nc.dram_tensor("bar4_out", [8, 4], F32).ap()
    settle_d = nc.dram_tensor("settle_d", [128, D], F32).ap()
    zrowB = nc.dram_tensor("zrowB", [4096, D], BF16).ap()
    outm = nc.dram_tensor("outm", [CPAD, D], BF16).ap()
    outall = nc.dram_tensor("outall", [NCORE * CPAD, D], BF16).ap()

    ctx = ExitStack()

    def SB(name, shape, dt, st=None):
        return (st or ctx).enter_context(nc.sbuf_tensor(name, shape, dt))

    wv_b = SB("wv_b", [128, FC, D], BF16)
    wo_b = SB("wo_b", [128, FC, D], BF16)
    gw_f = SB("gw_f", [128, FC, E], F32)
    onecolF = SB("onecolF", [128, 1], F32)
    gsum8 = SB("gsum8", [E, 1], F32)
    fr_s = SB("fr_s", [128, FC], F32)
    f2p_s = SB("f2p_s", [128, FC], F32)
    gb_s = SB("gb_s", [E, 1], F32)
    cid_s = SB("cid_s", [128, 1], F32)
    psel_s = SB("psel_s", [128, 16], F32)
    b1_s = SB("b1_s", [128, HC], F32)
    b2r_s = SB("b2r_s", [1, D], F32)
    csb_c = SB("csb_c", [128, FC, 256], F32)
    csb_s = SB("csb_s", [128, FC, 256], F32)
    csC = SB("csC", [128, FC, 16], F32)
    csS = SB("csS", [128, FC, 16], F32)
    csSn = SB("csSn", [128, FC, 16], F32)
    carr_c = SB("carr_c", [128, FC], F32)
    carr_s = SB("carr_s", [128, FC], F32)
    ones513 = SB("ones513", [128, 513], F32)
    onecol_b = SB("onecol_b", [128, 1], BF16)
    ones1f = SB("ones1f", [1, 128], F32)
    onesq = SB("onesq", [128, 128], F32)
    identf = SB("identf", [128, 128], F32)
    identb = SB("identb", [128, 128], BF16)
    l128b = SB("l128b", [128, 128], BF16)
    iotaR = SB("iotaR", [128, 128], F32)
    iotaC = SB("iotaC", [128, 128], F32)
    iotaU = SB("iotaU", [128, 256], F32)
    iota256 = SB("iota256", [128, 256], F32)
    iotaB16 = SB("iotaB16", [128, 16], F32)
    zerS = SB("zerS", [128, 96], F32)
    lsb = SB("lsb", [128, NTILE, 2], F32)
    gidxI = SB("gidxI", [128, NTILE], I32)
    myb = SB("myb", [128, 4, 2, 16], F32)
    g0I = SB("g0I", [128, 2, 16], I32)
    g1I = SB("g1I", [128, 2, 16], I32)
    b2b_s = SB("b2b_s", [128, D], F32)

    PS = [ctx.enter_context(nc.psum_tensor(f"ps{k}", [128, 512], F32)) for k in range(8)]

    SCHED = {"sy": [], "sc": [], "ve": [], "te": [], "gp": []}
    CNT = {k: 0 for k in ["dsy", "dsc", "dgp", "sc", "ve", "te", "gp", "cc"]}
    SEQ = [0]

    def add(eng, fn, waits=(), inc=None, n=1):
        if inc is not None:
            CNT[inc] += n
        SEQ[0] += 1
        SCHED[eng].append((SEQ[0], tuple((s, v) for (s, v) in waits if v > 0), inc, n, fn))

    def cur(s):
        return CNT[s]

    add("gp", lambda e: e.iota(iotaU[:], pattern=[[1, 256]], base=0,
                               channel_multiplier=256, allow_small_or_imprecise_dtypes=True))
    add("gp", lambda e: e.iota(iota256[:], pattern=[[1, 256]], base=0,
                               channel_multiplier=0, allow_small_or_imprecise_dtypes=True))
    add("gp", lambda e: e.iota(iotaB16[:], pattern=[[256, 16]], base=0,
                               channel_multiplier=0, allow_small_or_imprecise_dtypes=True),
        inc="gp")
    W_IOTA = cur("gp")

    add("ve", lambda e: e.memset(ones513[:], 1.0))
    add("ve", lambda e: e.memset(onecol_b[:], 1.0))
    add("ve", lambda e: e.memset(onecolF[:], 1.0))
    add("ve", lambda e: e.memset(ones1f[:], 1.0))
    add("ve", lambda e: e.memset(onesq[:], 1.0))
    add("ve", lambda e: e.memset(zerS[:], 0.0))
    add("ve", lambda e: e.memset(carr_c[:], 0.0))
    add("ve", lambda e: e.memset(carr_s[:], 0.0), inc="ve")
    W_MEMS = cur("ve")

    add("gp", lambda e: e.affine_select(identf[:], onesq[:], pattern=[[-1, 128]],
                                        compare_op=ALU.is_equal, fill=0.0, channel_multiplier=1),
        waits=[("ve", W_MEMS)])
    add("gp", lambda e: e.affine_select(identb[:], onesq[:], pattern=[[-1, 128]],
                                        compare_op=ALU.is_equal, fill=0.0, channel_multiplier=1))
    add("gp", lambda e: e.iota(iotaR[:], pattern=[[0, 128]], base=0,
                               channel_multiplier=1, allow_small_or_imprecise_dtypes=True))
    add("gp", lambda e: e.iota(iotaC[:], pattern=[[1, 128]], base=0,
                               channel_multiplier=0, allow_small_or_imprecise_dtypes=True),
        inc="gp")
    W_CONST = cur("gp")
    add("ve", lambda e: e.tensor_tensor(l128b[:], iotaR[:], iotaC[:], op=ALU.is_lt),
        waits=[("gp", W_CONST)], inc="ve")
    W_L128 = cur("ve")

    add("sy", lambda e: e.dma_start(out=fr_s[:], in_=fr_in), inc="dsy", n=16)
    add("sy", lambda e: e.dma_start(out=gb_s[:], in_=gb_in), inc="dsy", n=16)
    add("sy", lambda e: e.dma_start(out=cid_s[:], in_=cid_in), inc="dsy", n=16)
    add("sy", lambda e: e.dma_start(out=psel_s[:], in_=psel_in), inc="dsy", n=16)
    add("sy", lambda e: e.dma_start(out=b1_s[:], in_=b1_in), inc="dsy", n=16)
    add("sy", lambda e: e.dma_start(out=b2r_s[:], in_=b2_in), inc="dsy", n=16)
    W_SMALL = cur("dsy")

    add("gp", lambda e: e.dma_start(out=wv_b[:], in_=wvT_in), inc="dgp", n=16)
    add("gp", lambda e: e.dma_start(out=wo_b[:], in_=woT_in), inc="dgp", n=16)
    W_WATT = cur("dgp")
    add("sy", lambda e: e.dma_start(out=gw_f[:], in_=gw_in), inc="dsy", n=16)

    add("gp", lambda e: e.dma_start(out=zrowB[0:512, :], in_=zin_in), inc="dgp", n=16)
    Z0 = cur("dgp")
    add("sc", lambda e: e.dma_start(out=zrowB[512:1024, :], in_=zrowB[0:512, :]),
        waits=[("dgp", Z0)], inc="dsc", n=16)
    add("sc", lambda e: e.dma_start(out=zrowB[1024:2048, :], in_=zrowB[0:1024, :]),
        waits=[("dsc", cur("dsc"))], inc="dsc", n=16)
    add("sc", lambda e: e.dma_start(out=zrowB[2048:4096, :], in_=zrowB[0:2048, :]),
        waits=[("dsc", cur("dsc"))], inc="dsc", n=16)
    ZF = cur("dsc")
    for zi in range(8):
        add("sc", lambda e, zi=zi: e.dma_start(
            out=h2all[4096 * zi:4096 * (zi + 1), :], in_=zrowB),
            waits=[("dsc", ZF)], inc="dsc", n=16)
    for zi in range(10):
        add("sc", lambda e, zi=zi: e.dma_start(
            out=outall[4096 * zi:4096 * (zi + 1), :], in_=zrowB),
            waits=[("dsc", ZF)], inc="dsc", n=16)
    add("sc", lambda e: e.dma_start(out=outall[40960:41984, :], in_=zrowB[0:1024, :]),
        waits=[("dsc", ZF)], inc="dsc", n=16)
    add("sc", lambda e: e.dma_start(
        out=lgall.rearrange("(a b) e -> a (b e)", b=128), in_=zin_in[0:256, :]),
        inc="dsc", n=16)
    W_GWF = cur("dsy")
    W_W1 = cur("dgp")

    attn_st = ExitStack()
    xcb = SB("xcb", [128, FC, CHT], F32, attn_st)
    xbb = SB("xbb", [128, FC, CHT], BF16, attn_st)
    xnb = SB("xnb", [128, FC, CHT], BF16, attn_st)
    rowscr = SB("rowscr", [1, 4, CHT], F32, attn_st)
    tscr = SB("tscr", [128, CHT], F32, attn_st)
    c_f = SB("c_f", [128, CHT], F32, attn_st)
    s_f = SB("s_f", [128, CHT], F32, attn_st)
    tre1 = SB("tre1", [128, CHT], F32, attn_st)
    tre2 = SB("tre2", [128, CHT], F32, attn_st)
    trec1 = SB("trec1", [128, CHT], F32, attn_st)
    trec2 = SB("trec2", [128, CHT], F32, attn_st)
    ain = SB("ain", [128, 513], F32, attn_st)
    Aout = SB("Aout", [128, 513], F32, attn_st)
    Bout = SB("Bout", [128, 513], F32, attn_st)
    rb = SB("rb", [128, FC, CHT], BF16, attn_st)
    x1b = SB("x1b", [128, FC, CHT], F32, attn_st)
    lgrow = SB("lgrow", [E, CHT], F32, attn_st)
    lgr8 = SB("lgr8", [E, 2, CHT], F32, attn_st)
    lgtt = SB("lgtt", [128, 32], F32, attn_st)
    evrow = SB("evrow", [128, 1024], F32, attn_st)
    evrowb = SB("evrowb", [128, 1024], BF16, attn_st)
    rotscr2 = SB("rotscr2", [128, FC, 16], F32, attn_st)
    rotk2 = SB("rotk2", [128, FC, 16], F32, attn_st)
    rotscr = xcb[:, :, 0:256]
    rotk = xcb[:, :, 256:512]
    w2st = xbb[:].rearrange("p a b -> p (a b)").rearrange("p (a d) -> p a d", a=4)

    w1st = xnb[:].rearrange("p a b -> p (a b)")[:, 0:1024].rearrange("p (a b) -> p a b", a=FC)
    prev = W_W1
    for hc in range(HC):
        w = [("dgp", prev)]
        add("gp", lambda e, hc=hc: e.dma_start(
            out=w1st, in_=w1_in[:, hc * 128:(hc + 1) * 128].rearrange("(a p) h -> p a h", p=128)),
            waits=w, inc="dgp", n=16)
        m_l = cur("dgp")
        add("gp", lambda e, hc=hc: e.dma_start(out=w1b[hc], in_=w1st),
            waits=[("dgp", m_l)], inc="dgp", n=16)
        prev = cur("dgp")
    W_W1B = cur("dgp")

    prev = W_W1B
    for j in range(8):
        w = [("dgp", prev)]
        add("gp", lambda e, j=j: e.dma_start(
            out=w2st, in_=w2_in[j * 512:(j + 1) * 512, :].rearrange("(a p) d -> p a d", p=128)),
            waits=w, inc="dgp", n=16)
        m_l = cur("dgp")
        add("gp", lambda e, j=j: e.dma_start(
            out=w2b[j * 512:(j + 1) * 512, :].rearrange("(a p) d -> p a d", p=128), in_=w2st),
            waits=[("dgp", m_l)], inc="dgp", n=16)
        prev = cur("dgp")
    W_W2 = cur("dgp")

    add("ve", lambda e: e.tensor_scalar(f2p_s[:], fr_s[:], 1.0 / TWO_PI, None, op0=ALU.mult),
        waits=[("dsy", W_SMALL)], inc="ve")
    W_F2P = cur("ve")

    def rotor_tables(iot, n, scr, kb, outc, outs):
        for fc in range(FC):
            add("sc", lambda e, fc=fc: e.activation(out=scr[:, fc, :], in_=iot[:, 0:n],
                                                    func=ACT.Copy, scale=f2p_s[:, fc:fc + 1]),
                waits=[("ve", W_F2P), ("gp", W_IOTA)] if fc == 0 else [])
        add("sc", lambda e: None, inc="sc")
        m0 = cur("sc")
        add("ve", lambda e: e.tensor_scalar(kb[:], scr[:], MAGIC, -MAGIC,
                                            op0=ALU.add, op1=ALU.add),
            waits=[("sc", m0)], inc="ve")
        mk = cur("ve")
        for fc in range(FC):
            add("sc", lambda e, fc=fc: e.activation(out=scr[:, fc, :], in_=iot[:, 0:n],
                                                    func=ACT.Copy, scale=fr_s[:, fc:fc + 1]),
                waits=[("ve", mk)] if fc == 0 else [])
        add("sc", lambda e: None, inc="sc")
        m1 = cur("sc")
        add("ve", lambda e: e.scalar_tensor_tensor(scr[:], kb[:], -CW1, scr[:],
                                                   op0=ALU.mult, op1=ALU.add),
            waits=[("sc", m1)])
        add("ve", lambda e: e.scalar_tensor_tensor(scr[:], kb[:], -CW2, scr[:],
                                                   op0=ALU.mult, op1=ALU.add))
        add("ve", lambda e: e.scalar_tensor_tensor(scr[:], kb[:], -CW3, scr[:],
                                                   op0=ALU.mult, op1=ALU.add))
        add("ve", lambda e: e.tensor_scalar(kb[:], scr[:], PI, None, op0=ALU.min))
        add("ve", lambda e: e.tensor_scalar(kb[:], kb[:], -PI, None, op0=ALU.max), inc="ve")
        m2 = cur("ve")
        add("sc", lambda e: e.activation(out=outs[:], in_=kb[:], func=ACT.Sin),
            waits=[("ve", m2)], inc="sc")
        m3 = cur("sc")
        add("ve", lambda e: e.tensor_scalar(kb[:], scr[:], PI / 2, None, op0=ALU.add),
            waits=[("sc", m3)])
        add("ve", lambda e: e.tensor_scalar(scr[:], kb[:], PI, None, op0=ALU.is_gt))
        add("ve", lambda e: e.scalar_tensor_tensor(kb[:], scr[:], -TWO_PI, kb[:],
                                                   op0=ALU.mult, op1=ALU.add))
        add("ve", lambda e: e.tensor_scalar(kb[:], kb[:], PI, None, op0=ALU.min))
        add("ve", lambda e: e.tensor_scalar(kb[:], kb[:], -PI, None, op0=ALU.max), inc="ve")
        m4 = cur("ve")
        add("sc", lambda e: e.activation(out=outc[:], in_=kb[:], func=ACT.Sin),
            waits=[("ve", m4)], inc="sc")

    rotor_tables(iota256, 256, rotscr.rearrange("p a b -> p a b"), rotk.rearrange("p a b -> p a b"),
                 csb_c, csb_s)
    rotor_tables(iotaB16, 16, rotscr2, rotk2, csC, csS)
    add("ve", lambda e: e.tensor_scalar(csSn[:], csS[:], -1.0, None, op0=ALU.mult),
        waits=[("sc", cur("sc"))], inc="ve")
    W_ROT = cur("ve")
    W_ROTSC = cur("sc")

    add("te", lambda e: e.matmul(PS[6][:, :], ones1f[:, :], b2r_s[0:1, 0:512],
                                 start=True, stop=True),
        waits=[("dsy", W_SMALL), ("ve", W_MEMS)])
    add("te", lambda e: e.matmul(PS[7][:, :], ones1f[:, :], b2r_s[0:1, 512:1024],
                                 start=True, stop=True), inc="te")
    W_B2MM = cur("te")
    add("ve", lambda e: e.tensor_copy(b2b_s[:, 0:512], PS[6][:, :]), waits=[("te", W_B2MM)])
    add("ve", lambda e: e.tensor_copy(b2b_s[:, 512:1024], PS[7][:, :]), inc="ve")
    W_B2B = cur("ve")
    for fc in range(FC):
        add("te", lambda e, fc=fc: e.matmul(PS[0][0:8, 0:1], gw_f[:, fc, :], onecolF[:, :],
                                            start=(fc == 0), stop=(fc == FC - 1)),
            waits=[("dsy", W_GWF), ("ve", W_B2B)] if fc == 0 else [])
    add("te", lambda e: None, inc="te")
    W_GSMM = cur("te")
    add("ve", lambda e: e.tensor_copy(gsum8[:], PS[0][0:8, 0:1]), waits=[("te", W_GSMM)], inc="ve")
    W_GSUM = cur("ve")

    prev_x_free = W_ROT
    prev_xbb_free = ("dgp", W_W2)
    prev_ps45_sc = 0
    prev_ps5_ve = 0
    prev_lg_dma = 0
    prev_spill_dma = 0
    prev_h2_dma = 0

    def ln_stats(xf32, xb16, sqb16, wv, wt):
        add("sc", lambda e: e.activation(out=xb16[:], in_=xf32[:], func=ACT.Copy), waits=wv)
        add("sc", lambda e: e.activation(out=sqb16[:], in_=xf32[:], func=ACT.Square), inc="sc")
        m0 = cur("sc")
        for fc in range(FC):
            add("te", lambda e, fc=fc: e.matmul(PS[4][0:1, :], onecol_b[:, :], xb16[:, fc, :],
                                                start=(fc == 0), stop=(fc == FC - 1)),
                waits=([("sc", m0)] + wt) if fc == 0 else [])
        for fc in range(FC):
            add("te", lambda e, fc=fc: e.matmul(PS[5][0:1, :], onecol_b[:, :], sqb16[:, fc, :],
                                                start=(fc == 0), stop=(fc == FC - 1)))
        add("te", lambda e: None, inc="te")
        m1 = cur("te")
        add("ve", lambda e: e.tensor_scalar(rowscr[0:1, 0, :], PS[4][0:1, :], 1.0 / D, None, op0=ALU.mult),
            waits=[("te", m1)])
        add("ve", lambda e: e.tensor_scalar(rowscr[0:1, 1, :], PS[5][0:1, :], 1.0 / D, None, op0=ALU.mult))
        add("ve", lambda e: e.tensor_tensor(rowscr[0:1, 2, :], rowscr[0:1, 0, :], rowscr[0:1, 0, :],
                                            op=ALU.mult))
        add("ve", lambda e: e.tensor_tensor(rowscr[0:1, 1, :], rowscr[0:1, 1, :], rowscr[0:1, 2, :],
                                            op=ALU.subtract))
        add("ve", lambda e: e.tensor_scalar(rowscr[0:1, 1, :], rowscr[0:1, 1, :], EPS, None, op0=ALU.add),
            inc="ve")
        m2 = cur("ve")
        add("sc", lambda e: e.activation(out=rowscr[0:1, 2, :], in_=rowscr[0:1, 1, :], func=ACT.Ln),
            waits=[("ve", m2)])
        add("sc", lambda e: e.activation(out=rowscr[0:1, 3, :], in_=rowscr[0:1, 2, :], func=ACT.Exp,
                                         scale=-0.5), inc="sc")
        m3 = cur("sc")
        add("ve", lambda e: e.scalar_tensor_tensor(rowscr[0:1, 1, :], rowscr[0:1, 0, :], -1.0,
                                                   rowscr[0:1, 3, :], op0=ALU.mult, op1=ALU.mult),
            waits=[("sc", m3)], inc="ve")
        m4 = cur("ve")
        add("te", lambda e: e.matmul(PS[6][:, :], ones1f[:, :], rowscr[0:1, 3, :], start=True, stop=True),
            waits=[("ve", m4)])
        add("te", lambda e: e.matmul(PS[7][:, :], ones1f[:, :], rowscr[0:1, 1, :], start=True, stop=True),
            inc="te")
        return cur("te")

    for i in range(NCH):
        add("sy", lambda e, i=i: e.dma_start(out=xcb[:], in_=xT_in[:, :, i * CHT:(i + 1) * CHT]),
            waits=[("ve", prev_x_free)], inc="dsy", n=16)
        m_x = cur("dsy")

        wv = [("dsy", m_x), ("ve", W_B2B)]
        if i == 0:
            wv.append(prev_xbb_free)
        wt = [("te", W_B2MM)]
        if prev_ps45_sc:
            wt = [("sc", prev_ps45_sc), ("ve", prev_ps5_ve)]
        m_bc1 = ln_stats(xcb, xbb, xnb, wv, wt)
        for fc in range(FC):
            add("ve", lambda e, fc=fc: e.tensor_tensor(tscr[:], xcb[:, fc, :], PS[6][:, :], op=ALU.mult),
                waits=[("te", m_bc1)] if fc == 0 else [])
            add("ve", lambda e, fc=fc: e.tensor_tensor(xnb[:, fc, :], tscr[:], PS[7][:, :], op=ALU.add))
        add("ve", lambda e: None, inc="ve")
        m_xn = cur("ve")

        m_cf_prev = 0
        m_chains = 0
        for g in range(2):
            bank_ids = list(range(0, 4)) if g == 0 else list(range(4, 8))
            for k, dc in enumerate(range(g * 4, g * 4 + 4)):
                for fc in range(FC):
                    w = [("ve", m_xn)] if (k == 0 and fc == 0) else []
                    add("te", lambda e, dc=dc, fc=fc, b=bank_ids[k]: e.matmul(
                        PS[b][:, :], wv_b[:, fc, dc * 128:(dc + 1) * 128], xnb[:, fc, :],
                        start=(fc == 0), stop=(fc == FC - 1)), waits=w)
            add("te", lambda e: None, inc="te")
            m_group = cur("te")

            for k, dc in enumerate(range(g * 4, g * 4 + 4)):
                b = bank_ids[k]
                for h in range(2):
                    bb = 2 * i + h
                    sl = slice(h * 256, (h + 1) * 256)
                    w = []
                    if m_cf_prev:
                        w = [("ve", m_cf_prev)]
                    elif i == 0 and dc == 0 and h == 0:
                        w = [("ve", W_ROT)]
                    add("sc", lambda e, dc=dc, bb=bb, sl=sl: e.activation(
                        out=trec1[:, sl], in_=csb_c[:, dc, :], func=ACT.Copy,
                        scale=csC[:, dc, bb:bb + 1]), waits=w if h == 0 else [])
                    add("sc", lambda e, dc=dc, bb=bb, sl=sl: e.activation(
                        out=trec2[:, sl], in_=csb_s[:, dc, :], func=ACT.Copy,
                        scale=csC[:, dc, bb:bb + 1]))
                add("sc", lambda e: None, inc="sc")
                m_rec = cur("sc")
                for h in range(2):
                    bb = 2 * i + h
                    sl = slice(h * 256, (h + 1) * 256)
                    add("ve", lambda e, dc=dc, bb=bb, sl=sl: e.scalar_tensor_tensor(
                        c_f[:, sl], csb_s[:, dc, :], csSn[:, dc, bb:bb + 1], trec1[:, sl],
                        op0=ALU.mult, op1=ALU.add),
                        waits=[("sc", m_rec)] if h == 0 else [])
                    add("ve", lambda e, dc=dc, bb=bb, sl=sl: e.scalar_tensor_tensor(
                        s_f[:, sl], csb_c[:, dc, :], csS[:, dc, bb:bb + 1], trec2[:, sl],
                        op0=ALU.mult, op1=ALU.add))
                add("ve", lambda e: None, inc="ve")
                m_cf_prev = cur("ve")

                add("ve", lambda e, dc=dc: e.tensor_copy(ain[:, 0:1], carr_c[:, dc:dc + 1]),
                    waits=[("te", m_group)])
                add("ve", lambda e, b=b: e.tensor_tensor(ain[:, 1:513], PS[b][:, :], c_f[:, :],
                                                         op=ALU.mult))
                add("ve", lambda e: e.tensor_tensor_scan(Aout[:, :], ones513[:, :], ain[:, :], 0.0,
                                                         op0=ALU.mult, op1=ALU.add), inc="ve")
                m_sa = cur("ve")
                add("ve", lambda e, dc=dc: e.tensor_copy(carr_c[:, dc:dc + 1], Aout[:, 512:513]),
                    waits=[("ve", m_sa)])
                add("ve", lambda e, dc=dc: e.tensor_copy(ain[:, 0:1], carr_s[:, dc:dc + 1]))
                add("ve", lambda e, b=b: e.tensor_tensor(ain[:, 1:513], PS[b][:, :], s_f[:, :],
                                                         op=ALU.mult))
                add("ve", lambda e: e.tensor_tensor_scan(Bout[:, :], ones513[:, :], ain[:, :], 0.0,
                                                         op0=ALU.mult, op1=ALU.add), inc="ve")
                m_sb = cur("ve")
                add("ve", lambda e, dc=dc: e.tensor_copy(carr_s[:, dc:dc + 1], Bout[:, 512:513]),
                    waits=[("ve", m_sb)])
                add("ve", lambda e: e.tensor_tensor(tre1[:, :], c_f[:, :], Aout[:, 1:513], op=ALU.mult))
                add("ve", lambda e: e.tensor_tensor(tre2[:, :], s_f[:, :], Bout[:, 1:513], op=ALU.mult))
                add("ve", lambda e, dc=dc: e.tensor_tensor(rb[:, dc, :], tre1[:, :], tre2[:, :],
                                                           op=ALU.add), inc="ve")
            m_chains = cur("ve")

        for g in range(2):
            bank_ids = list(range(0, 4)) if g == 0 else list(range(4, 8))
            for k, dc in enumerate(bank_ids):
                for kc in range(FC):
                    w = [("ve", m_chains)] if (g == 0 and k == 0 and kc == 0) else []
                    add("te", lambda e, dc=dc, kc=kc: e.matmul(
                        PS[dc][:, :], wo_b[:, kc, dc * 128:(dc + 1) * 128], rb[:, kc, :],
                        start=(kc == 0), stop=(kc == FC - 1)), waits=w)
        add("te", lambda e: None, inc="te")
        m_omm = cur("te")
        for dc in range(FC):
            add("ve", lambda e, dc=dc: e.tensor_tensor(x1b[:, dc, :], xcb[:, dc, :], PS[dc][:, :],
                                                       op=ALU.add),
                waits=[("te", m_omm)] if dc == 0 else [])
        add("ve", lambda e: None, inc="ve")
        m_x1 = cur("ve")
        prev_x_free = m_x1

        m_bc2 = ln_stats(x1b, xbb, xnb, [("ve", m_x1)], [("ve", m_x1)])
        for fc in range(FC):
            add("ve", lambda e, fc=fc: e.tensor_tensor(tscr[:], x1b[:, fc, :], PS[6][:, :], op=ALU.mult),
                waits=[("te", m_bc2)] if fc == 0 else [])
            add("ve", lambda e, fc=fc: e.tensor_tensor(xnb[:, fc, :], tscr[:], PS[7][:, :], op=ALU.add))
        add("ve", lambda e: None, inc="ve")
        m_h2 = cur("ve")

        for fc in range(FC):
            add("te", lambda e, fc=fc: e.matmul(PS[4][0:8, :], gw_f[:, fc, :], x1b[:, fc, :],
                                                start=(fc == 0), stop=(fc == FC - 1)),
                waits=[("ve", m_h2), ("te", W_GSMM)] if fc == 0 else [])
        add("te", lambda e: e.matmul(PS[5][0:8, :], ones1f[0:1, 0:8], rowscr[0:1, 3, :],
                                     start=True, stop=True))
        add("te", lambda e: e.matmul(PS[6][0:8, :], ones1f[0:1, 0:8], rowscr[0:1, 1, :],
                                     start=True, stop=True))
        add("te", lambda e: None, inc="te")
        m_lg = cur("te")
        add("sc", lambda e: e.activation(out=lgrow[:, :], in_=PS[4][0:8, :], func=ACT.Copy),
            waits=[("te", m_lg)], inc="sc")
        m_lgr = cur("sc")
        prev_ps45_sc = m_lgr
        add("ve", lambda e: e.tensor_copy(lgr8[:, 0, :], PS[5][0:8, :]), waits=[("sc", m_lgr)])
        add("ve", lambda e: e.tensor_copy(lgr8[:, 1, :], PS[6][0:8, :]), inc="ve")
        m_lg8 = cur("ve")
        add("ve", lambda e: e.tensor_tensor(lgrow[:, :], lgrow[:, :], lgr8[:, 0, :], op=ALU.mult),
            waits=[("ve", m_lg8)])
        add("ve", lambda e: e.scalar_tensor_tensor(lgrow[:, :], lgr8[:, 1, :], gsum8[0:8, 0:1],
                                                   lgrow[:, :], op0=ALU.mult, op1=ALU.add))
        add("ve", lambda e: e.tensor_scalar(lgrow[:, :], lgrow[:, :], gb_s[0:8, 0:1], None,
                                            op0=ALU.add), inc="ve")
        m_lgv = cur("ve")
        for tt in range(4):
            w = [("ve", m_lgv)]
            if tt == 0:
                w.append(("gp", W_CONST))
            add("te", lambda e, tt=tt: e.matmul(PS[5][:, tt * 8:(tt + 1) * 8],
                                                   lgrow[0:8, tt * 128:(tt + 1) * 128],
                                                   identf[0:8, 0:8], start=True, stop=True),
                waits=w if tt == 0 else [])
        add("te", lambda e: None, inc="te")
        m_lgt = cur("te")
        add("ve", lambda e: e.tensor_copy(lgtt[:, :], PS[5][:, 0:32]),
            waits=[("te", m_lgt), ("dsc", prev_lg_dma)], inc="ve")
        m_lgc = cur("ve")
        prev_ps5_ve = m_lgc
        for tt in range(4):
            add("sc", lambda e, i=i, tt=tt: e.dma_start(
                out=lgm[i * CHT + tt * 128:i * CHT + (tt + 1) * 128, :],
                in_=lgtt[:, tt * 8:(tt + 1) * 8]),
                waits=[("ve", m_lgc)] if tt == 0 else [], inc="dsc", n=16)
        prev_lg_dma = cur("dsc")

        m_ev = 0
        m_ev2 = 0
        for tt in range(4):
            for fc in range(FC):
                ps = PS[0] if fc < 4 else PS[1]
                w = []
                if fc == 0:
                    w = [("ve", m_h2)] if tt == 0 else [("ve", m_ev)]
                add("te", lambda e, fc=fc, tt=tt, ps=ps: e.matmul(
                    ps[:, (fc % 4) * 128:(fc % 4 + 1) * 128],
                    x1b[:, fc, tt * 128:(tt + 1) * 128], identf[:, :], start=True, stop=True), waits=w)
            add("te", lambda e: None, inc="te")
            m_tr = cur("te")
            add("ve", lambda e: e.tensor_copy(evrow[:, 0:512], PS[0][:, :]),
                waits=[("te", m_tr), ("dsc", prev_spill_dma)])
            add("ve", lambda e: e.tensor_copy(evrow[:, 512:1024], PS[1][:, :]), inc="ve")
            m_ev = cur("ve")
            add("sc", lambda e, i=i, tt=tt: e.dma_start(
                out=x1sp[i * CHT + tt * 128:i * CHT + (tt + 1) * 128, :], in_=evrow[:, :]),
                waits=[("ve", m_ev)], inc="dsc", n=16)
            prev_spill_dma = cur("dsc")
            for fc in range(FC):
                ps = PS[2] if fc < 4 else PS[3]
                w = [("ve", m_ev2)] if (fc == 0 and tt > 0) else []
                add("te", lambda e, fc=fc, tt=tt, ps=ps: e.matmul(
                    ps[:, (fc % 4) * 128:(fc % 4 + 1) * 128],
                    xnb[:, fc, tt * 128:(tt + 1) * 128], identb[:, :], start=True, stop=True), waits=w)
            add("te", lambda e: None, inc="te")
            m_tr2 = cur("te")
            add("ve", lambda e: e.tensor_copy(evrowb[:, 0:512], PS[2][:, :]),
                waits=[("te", m_tr2), ("dsc", prev_h2_dma)])
            add("ve", lambda e: e.tensor_copy(evrowb[:, 512:1024], PS[3][:, :]), inc="ve")
            m_ev2 = cur("ve")
            add("sc", lambda e, i=i, tt=tt: e.dma_start(
                out=h2m[i * CHT + tt * 128:i * CHT + (tt + 1) * 128, :], in_=evrowb[:, :]),
                waits=[("ve", m_ev2)], inc="dsc", n=16)
            prev_h2_dma = cur("dsc")

    A_DONE_VE = cur("ve")
    A_DONE_DSC = cur("dsc")

    MARK1 = {k: len(v) for k, v in SCHED.items()}
    add("gp", lambda e: e.collective_compute(
        "AllGather", ALU.bypass, replica_groups=[list(range(NCORE))],
        ins=[lgm.opt()], outs=[lgall.opt()]),
        waits=[("dsc", A_DONE_DSC)], inc="cc")
    CC_LG = cur("cc")
    add("gp", lambda e: e.collective_compute(
        "AllGather", ALU.bypass, replica_groups=[list(range(NCORE))],
        ins=[h2m.opt()], outs=[h2all.opt()]), inc="cc")
    CC_AGH = cur("cc")
    add("gp", lambda e: e.collective_compute(
        "AllReduce", ALU.add, replica_groups=[list(range(NCORE))],
        ins=[bar_in.opt()], outs=[bar_out.opt()]),
        waits=[("cc", CC_AGH)], inc="cc")
    CC_BAR1 = cur("cc")
    add("gp", lambda e: e.collective_compute(
        "AllReduce", ALU.add, replica_groups=[list(range(NCORE))],
        ins=[bar_in.opt()], outs=[bar2_out.opt()]),
        waits=[("cc", CC_BAR1)], inc="cc")
    CC_H2 = cur("cc")

    attn_st.close()
    rt_st = ExitStack()
    GL = SB("GL", [128, 256, E], F32, rt_st)
    T1 = SB("T1", [128, 256, E], F32, rt_st)
    mx = SB("mx", [128, 256], F32, rt_st)
    m1 = SB("m1", [128, 256], F32, rt_st)
    m2 = SB("m2", [128, 256], F32, rt_st)
    e1v = SB("e1v", [128, 256], F32, rt_st)
    e2v = SB("e2v", [128, 256], F32, rt_st)
    pn1 = SB("pn1", [128, 256], F32, rt_st)
    pn2 = SB("pn2", [128, 256], F32, rt_st)
    scr1 = SB("scr1", [128, 256], F32, rt_st)
    scr2 = SB("scr2", [128, 256], F32, rt_st)
    scr3 = SB("scr3", [128, 256], F32, rt_st)
    rank_all = SB("rank_all", [128, E, 256], F32, rt_st)
    exclc = SB("exclc", [128, 1], F32, rt_st)
    totb = SB("totb", [128, 1], BF16, rt_st)
    pairb = SB("pairb", [128, 256, 2], F32, rt_st)
    posi = SB("posi", [128, 256], I32, rt_st)
    s1m = SB("s1m", [128, 256], F32, rt_st)
    s2m = SB("s2m", [128, 256], F32, rt_st)
    rkme = SB("rkme", [128, 256], F32, rt_st)
    kpme = SB("kpme", [128, 256], F32, rt_st)
    wvme = SB("wvme", [128, 256], F32, rt_st)
    gk0 = SB("gk0", [128, 256], F32, rt_st)
    gk1 = SB("gk1", [128, 256], F32, rt_st)
    kk0 = SB("kk0", [128, 256], F32, rt_st)
    kk1 = SB("kk1", [128, 256], F32, rt_st)
    pk4 = SB("pk4", [128, 4, 256], F32, rt_st)
    my4 = SB("my4", [16, 4, 256], F32, rt_st)

    add("sy", lambda e: e.dma_start(out=GL[:], in_=lgall.rearrange("(p b) e -> p b e", p=128)),
        waits=[("cc", CC_BAR1)], inc="dsy", n=16)
    R_GL = cur("dsy")

    add("ve", lambda e: e.reduce_max(mx[:], GL[:], axis=AX),
        waits=[("dsy", R_GL), ("ve", A_DONE_VE)], inc="ve")
    add("ve", lambda e: None, waits=[("ve", cur("ve"))])
    for ee in range(E):
        add("ve", lambda e, ee=ee: e.tensor_tensor(T1[:, :, ee], GL[:, :, ee], mx[:], op=ALU.subtract))
    add("ve", lambda e: None, inc="ve")
    R_T1 = cur("ve")
    add("sc", lambda e: e.activation(out=GL[:], in_=T1[:], func=ACT.Exp),
        waits=[("ve", R_T1)], inc="sc")
    R_EXP = cur("sc")
    add("ve", lambda e: e.reduce_sum(mx[:], GL[:], axis=AX), waits=[("sc", R_EXP)], inc="ve")
    add("ve", lambda e: e.reciprocal(scr1[:], mx[:]), waits=[("ve", cur("ve"))])
    for ee in range(E):
        add("ve", lambda e, ee=ee: e.tensor_tensor(GL[:, :, ee], GL[:, :, ee], scr1[:], op=ALU.mult))
    add("ve", lambda e: e.reduce_max(m1[:], GL[:], axis=AX), inc="ve")
    add("ve", lambda e: e.memset(e1v[:], 8.0), waits=[("ve", cur("ve"))])
    for ee in range(E):
        add("ve", lambda e, ee=ee: e.tensor_tensor(scr1[:], GL[:, :, ee], m1[:], op=ALU.is_equal))
        add("ve", lambda e, ee=ee: e.tensor_scalar(scr1[:], scr1[:], float(ee - 8), 8.0,
                                                   op0=ALU.mult, op1=ALU.add))
        add("ve", lambda e, ee=ee: e.tensor_tensor(e1v[:], e1v[:], scr1[:], op=ALU.min))
    for ee in range(E):
        add("ve", lambda e, ee=ee: e.tensor_scalar(scr1[:], e1v[:], float(ee), None, op0=ALU.is_equal))
        add("ve", lambda e, ee=ee: e.tensor_scalar(scr1[:], scr1[:], -1.0, 1.0, op0=ALU.mult, op1=ALU.add))
        add("ve", lambda e, ee=ee: e.tensor_tensor(T1[:, :, ee], GL[:, :, ee], scr1[:], op=ALU.mult))
    add("ve", lambda e: e.reduce_max(m2[:], T1[:], axis=AX), inc="ve")
    add("ve", lambda e: e.memset(e2v[:], 8.0), waits=[("ve", cur("ve"))])
    for ee in range(E):
        add("ve", lambda e, ee=ee: e.tensor_tensor(scr1[:], T1[:, :, ee], m2[:], op=ALU.is_equal))
        add("ve", lambda e, ee=ee: e.tensor_scalar(scr1[:], scr1[:], float(ee - 8), 8.0,
                                                   op0=ALU.mult, op1=ALU.add))
        add("ve", lambda e, ee=ee: e.tensor_tensor(e2v[:], e2v[:], scr1[:], op=ALU.min))
    add("ve", lambda e: e.tensor_tensor(scr1[:], m1[:], m2[:], op=ALU.add), inc="ve")
    add("ve", lambda e: e.reciprocal(scr2[:], scr1[:]), waits=[("ve", cur("ve"))])
    add("ve", lambda e: e.tensor_tensor(pn1[:], m1[:], scr2[:], op=ALU.mult))
    add("ve", lambda e: e.tensor_tensor(pn2[:], m2[:], scr2[:], op=ALU.mult), inc="ve")
    R_TOP2 = cur("ve")

    for ee in range(E):
        add("ve", lambda e, ee=ee: e.tensor_scalar(scr1[:], e1v[:], float(ee), None, op0=ALU.is_equal))
        add("ve", lambda e, ee=ee: e.tensor_scalar(scr2[:], e2v[:], float(ee), None, op0=ALU.is_equal))
        add("ve", lambda e, ee=ee: e.tensor_tensor(scr1[:], scr1[:], scr2[:], op=ALU.add))
        add("ve", lambda e, ee=ee: e.tensor_tensor_scan(rank_all[:, ee, :], ones513[:, 0:256],
                                                        scr1[:], 0.0, op0=ALU.mult, op1=ALU.add))
        add("ve", lambda e, ee=ee: e.reduce_sum(exclc[:], scr1[:], axis=AX), inc="ve")
        m_r1 = cur("ve")
        add("ve", lambda e: e.tensor_copy(totb[:], exclc[:]), waits=[("ve", m_r1)], inc="ve")
        m_r2 = cur("ve")
        add("te", lambda e: e.matmul(PS[0][:, 0:1], l128b[:, :], totb[:, :], start=True, stop=True),
            waits=[("ve", m_r2)], inc="te")
        m_r3 = cur("te")
        add("ve", lambda e: e.tensor_copy(exclc[:], PS[0][:, 0:1]), waits=[("te", m_r3)], inc="ve")
        m_r4 = cur("ve")
        add("ve", lambda e, ee=ee: e.tensor_scalar(rank_all[:, ee, :], rank_all[:, ee, :],
                                                   exclc[:, 0:1], None, op0=ALU.add),
            waits=[("ve", m_r4)])
    add("ve", lambda e: None, inc="ve")
    R_RANK = cur("ve")

    add("ve", lambda e: e.tensor_scalar(s1m[:], e1v[:], cid_s[:, 0:1], None, op0=ALU.is_equal))
    add("ve", lambda e: e.tensor_scalar(s2m[:], e2v[:], cid_s[:, 0:1], None, op0=ALU.is_equal))
    add("ve", lambda e: e.memset(rkme[:], 0.0))
    for ee in range(E):
        add("ve", lambda e, ee=ee: e.tensor_scalar(exclc[:], cid_s[:, 0:1], float(ee), None,
                                                   op0=ALU.is_equal), inc="ve")
        m_c = cur("ve")
        add("ve", lambda e, ee=ee: e.tensor_scalar(scr1[:], rank_all[:, ee, :], exclc[:, 0:1], None,
                                                   op0=ALU.mult), waits=[("ve", m_c)])
        add("ve", lambda e: e.tensor_tensor(rkme[:], rkme[:], scr1[:], op=ALU.add))
    add("ve", lambda e: e.tensor_scalar(kpme[:], rkme[:], float(CAP), None, op0=ALU.is_le))
    add("ve", lambda e: e.tensor_tensor(scr1[:], s1m[:], pn1[:], op=ALU.mult))
    add("ve", lambda e: e.tensor_tensor(scr2[:], s2m[:], pn2[:], op=ALU.mult))
    add("ve", lambda e: e.tensor_tensor(wvme[:], scr1[:], scr2[:], op=ALU.add))
    add("ve", lambda e: e.tensor_tensor(wvme[:], wvme[:], kpme[:], op=ALU.mult))
    add("ve", lambda e: e.tensor_tensor(scr1[:], s1m[:], s2m[:], op=ALU.add))
    add("ve", lambda e: e.tensor_tensor(scr1[:], scr1[:], kpme[:], op=ALU.mult))
    add("ve", lambda e: e.tensor_scalar(scr2[:], rkme[:], -1.0, None, op0=ALU.add))
    add("ve", lambda e: e.tensor_scalar(scr3[:], scr2[:], 1.0 / 128.0, -0.49609375,
                                        op0=ALU.mult, op1=ALU.add))
    add("ve", lambda e: e.tensor_scalar(scr3[:], scr3[:], MAGIC2, -MAGIC2, op0=ALU.add, op1=ALU.add))
    add("ve", lambda e: e.scalar_tensor_tensor(scr2[:], scr3[:], -128.0, scr2[:],
                                               op0=ALU.mult, op1=ALU.add))
    add("ve", lambda e: e.scalar_tensor_tensor(scr2[:], scr2[:], 41.0, scr3[:],
                                               op0=ALU.mult, op1=ALU.add))
    add("ve", lambda e: e.tensor_scalar(scr2[:], scr2[:], float(-CPAD), None, op0=ALU.add))
    add("ve", lambda e: e.tensor_tensor(scr2[:], scr2[:], scr1[:], op=ALU.mult))
    add("ve", lambda e: e.tensor_scalar(scr2[:], scr2[:], float(CPAD), None, op0=ALU.add), inc="ve")
    add("ve", lambda e: e.tensor_copy(posi[:], scr2[:]), waits=[("ve", cur("ve"))])
    add("ve", lambda e: e.tensor_copy(pairb[:, :, 0], iotaU[:]))
    add("ve", lambda e: e.tensor_copy(pairb[:, :, 1], wvme[:]), inc="ve")
    R_PAIR = cur("ve")

    add("gp", lambda e: e.dma_start(out=listd[0:CPAD, :], in_=zerS[:, 0:82]),
        waits=[("ve", W_MEMS), ("dgp", W_W2)], inc="dgp", n=16)
    add("gp", lambda e: e.dma_start(out=listd[CPAD:CPAD + 2, :], in_=zerS[0:1, 0:4]),
        inc="dgp", n=16)
    R_LZ = cur("dgp")
    for bi in range(256):
        w = [("ve", R_PAIR), ("dgp", R_LZ)] if bi == 0 else []
        add("gp", lambda e, bi=bi: e.indirect_dma_start(
            out=listd[0:1, :], out_offset=IndirectOffsetOnAxis(posi[:, bi:bi + 1], 0),
            in_=pairb[:, bi, :], in_offset=None),
            waits=w, inc="dgp", n=16)
    R_SCAT = cur("dgp")

    add("sy", lambda e: e.dma_start(out=lsb[:], in_=listd[0:CPAD, :].rearrange("(p r) t -> p r t", p=128)),
        waits=[("dgp", R_SCAT)], inc="dsy", n=16)
    R_LSB = cur("dsy")
    add("ve", lambda e: e.tensor_copy(gidxI[:], lsb[:, :, 0]), waits=[("dsy", R_LSB)], inc="ve")
    R_GIDX = cur("ve")

    for (gk, kk, evec) in ((gk0, kk0, e1v), (gk1, kk1, e2v)):
        add("ve", lambda e, gk=gk: e.memset(gk[:], 0.0), waits=[("ve", R_RANK)])
        for ee in range(E):
            add("ve", lambda e, ee=ee, evec=evec: e.tensor_scalar(scr1[:], evec[:], float(ee), None,
                                                                  op0=ALU.is_equal))
            add("ve", lambda e, ee=ee: e.tensor_tensor(scr1[:], scr1[:], rank_all[:, ee, :], op=ALU.mult))
            add("ve", lambda e, gk=gk: e.tensor_tensor(gk[:], gk[:], scr1[:], op=ALU.add))
        add("ve", lambda e, gk=gk, kk=kk: e.tensor_scalar(kk[:], gk[:], float(CAP), None, op0=ALU.is_le))
        add("ve", lambda e, gk=gk, evec=evec: e.scalar_tensor_tensor(gk[:], evec[:], float(CPAD), gk[:],
                                                                     op0=ALU.mult, op1=ALU.add))
        add("ve", lambda e, gk=gk: e.tensor_scalar(gk[:], gk[:], -1.0, None, op0=ALU.add))
        add("ve", lambda e, gk=gk, kk=kk: e.tensor_tensor(gk[:], gk[:], kk[:], op=ALU.mult))
    add("ve", lambda e: e.tensor_copy(pk4[:, 0, :], gk0[:]))
    add("ve", lambda e: e.tensor_copy(pk4[:, 1, :], gk1[:]))
    add("ve", lambda e: e.tensor_copy(pk4[:, 2, :], kk0[:]))
    add("ve", lambda e: e.tensor_copy(pk4[:, 3, :], kk1[:]), inc="ve")
    R_PK4 = cur("ve")
    add("te", lambda e: e.matmul(PS[0][0:16, :], psel_s[:, :],
                                 pk4[:].rearrange("p a b -> p (a b)")[:, 0:512],
                                 start=True, stop=True),
        waits=[("ve", R_PK4)])
    add("te", lambda e: e.matmul(PS[1][0:16, :], psel_s[:, :],
                                 pk4[:].rearrange("p a b -> p (a b)")[:, 512:1024],
                                 start=True, stop=True), inc="te")
    R_PSEL = cur("te")
    add("ve", lambda e: e.tensor_copy(my4[0:16, 0:2, :],
                                      PS[0][0:16, :].rearrange("p (a b) -> p a b", a=2)),
        waits=[("te", R_PSEL)])
    add("ve", lambda e: e.tensor_copy(my4[0:16, 2:4, :],
                                      PS[1][0:16, :].rearrange("p (a b) -> p a b", a=2)), inc="ve")
    R_MY4 = cur("ve")
    for q in range(4):
        for bbk in range(2):
            w = [("ve", R_MY4)] if (q == 0 and bbk == 0) else []
            add("te", lambda e, q=q, bbk=bbk: e.matmul(
                PS[2][:, (q * 2 + bbk) * 16:(q * 2 + bbk + 1) * 16],
                my4[0:16, q, bbk * 128:(bbk + 1) * 128], identf[0:16, 0:16], start=True, stop=True), waits=w)
    add("te", lambda e: None, inc="te")
    R_MYT = cur("te")
    for q in range(4):
        for bbk in range(2):
            add("ve", lambda e, q=q, bbk=bbk: e.tensor_copy(
                myb[:, q, bbk, :], PS[2][:, (q * 2 + bbk) * 16:(q * 2 + bbk + 1) * 16]),
                waits=[("te", R_MYT)] if (q == 0 and bbk == 0) else [])
    add("ve", lambda e: None, inc="ve")
    add("ve", lambda e: e.tensor_copy(g0I[:], myb[:, 0, :, :]), waits=[("ve", cur("ve"))])
    add("ve", lambda e: e.tensor_copy(g1I[:], myb[:, 1, :, :]), inc="ve")
    R_BACK = cur("ve")
    MARK2 = {k: len(v) for k, v in SCHED.items()}

    rt_st.close()

    ffn_st = ExitStack()
    xgt = SB("xgt", [128, 2, D], BF16, ffn_st)
    w1s = SB("w1s", [128, 2, FC, 128], BF16, ffn_st)
    XINf = SB("XINf", [128, FC, 512], BF16, ffn_st)
    Hb = SB("Hb", [128, HC, 512], BF16, ffn_st)
    w2s = SB("w2s", [128, 2, D], BF16, ffn_st)
    t32 = SB("t32", [128, 512], F32, ffn_st)
    OUTt = SB("OUTt", [128, 2, D], BF16, ffn_st)

    m_xin_te = {}
    m_w1_cons = {}
    m_w2_cons = {}
    m_mm2_done = 0
    m_relu = {}
    m_outdma = {}
    hcg = 0

    KSETTLE = int(os.environ.get("KSETTLE", "16"))
    if KSETTLE:
        for si in range(KSETTLE):
            sw = [("cc", CC_H2)] if si == 0 else [("dgp", cur("dgp"))]
            add("gp", lambda e: e.dma_start(out=settle_d, in_=x1sp[0:128, :]),
                waits=sw, inc="dgp", n=16)

    last_buf_use = {0: None, 1: None}
    BLOCKS_ALL = BLOCKS + [(0, 4), (4, 4)]
    for b, (r0, ntk) in enumerate(BLOCKS_ALL):
        nb = ntk * 128
        m_xinf_parts = []
        for k in range(ntk):
            r = r0 + k
            w = []
            if b == 0 and r == 0:
                w = [("cc", CC_H2), ("ve", R_GIDX)]
            if last_buf_use[r % 2] is not None:
                w.append(("te", last_buf_use[r % 2]))
            if debug and b == 0 and r in (2, 3):
                w.append(("dsc", m_dump_dsc))
            if debug and b == 1 and r == 4:
                w.append(("dsc", m_dump3_dsc))
            add("gp", lambda e, r=r: e.indirect_dma_start(
                out=xgt[:, r % 2, :], out_offset=None,
                in_=h2all[0:1, :], in_offset=IndirectOffsetOnAxis(gidxI[:, r:r + 1], 0)),
                waits=w, inc="dgp", n=16)
            m_g = cur("dgp")
            if debug and b == 0 and r in (0, 1):
                add("sc", lambda e, r=r: e.dma_start(
                    out=dbg_xg[:, r, :], in_=xgt[:, r % 2, :]),
                    waits=[("dgp", m_g)], inc="dsc", n=16)
                if r == 0:
                    add("sc", lambda e: e.dma_start(out=dbg_gi, in_=gidxI[:]),
                        inc="dsc", n=16)
                m_dump_dsc = cur("dsc")
            for fc in range(FC):
                ps = PS[4] if fc < 4 else PS[5]
                w2_ = []
                if fc == 0:
                    w2_ = [("dgp", m_g)]
                    if k == 0 and b > 0:
                        w2_.append(("ve", m_mm2_done))
                    if k > 0:
                        w2_.append(("ve", m_xinf_parts[-1]))
                    if b == 0 and r == 0:
                        w2_.append(("ve", R_BACK))
                        w2_.append(("sc", prev_ps45_sc))
                add("te", lambda e, fc=fc, r=r, ps=ps: e.matmul(
                    ps[:, (fc % 4) * 128:(fc % 4 + 1) * 128],
                    xgt[:, r % 2, fc * 128:(fc + 1) * 128], identb[:, :], start=True, stop=True), waits=w2_)
            add("te", lambda e: None, inc="te")
            m_xin_te[r] = cur("te")
            last_buf_use[r % 2] = m_xin_te[r]
            add("ve", lambda e, k=k: e.tensor_copy(
                XINf[:, 0:4, k * 128:(k + 1) * 128],
                PS[4][:, :].rearrange("p (a b) -> p a b", a=4)),
                waits=[("te", m_xin_te[r])])
            add("ve", lambda e, k=k: e.tensor_copy(
                XINf[:, 4:8, k * 128:(k + 1) * 128],
                PS[5][:, :].rearrange("p (a b) -> p a b", a=4)), inc="ve")
            m_xinf_parts.append(cur("ve"))
        m_xinf = m_xinf_parts[-1]
        if debug and b == 0:
            add("sc", lambda e: e.dma_start(out=dbg_xi, in_=XINf[:, :, 0:128]),
                waits=[("ve", m_xinf)], inc="dsc", n=16)
            m_dump2_dsc = cur("dsc")

        for hc in range(HC):
            bank = PS[hc % 4]
            hcg1 = b * HC + hc
            w = [("te", m_w1_cons[hcg1 - 2])] if hcg1 >= 2 else [("dgp", W_W1B)]
            add("sy", lambda e, hc=hc, hcg1=hcg1: e.dma_start(
                out=w1s[:, hcg1 % 2, :, :], in_=w1b[hc]),
                waits=w, inc="dsy", n=16)
            m_w1l = cur("dsy")
            for fc in range(FC):
                w = []
                if fc == 0:
                    w = [("dsy", m_w1l)]
                    if hc == 0:
                        w.append(("ve", m_xinf))
                    if hc >= 4:
                        w.append(("sc", m_relu[(b, hc - 4)]))
                add("te", lambda e, hc=hc, fc=fc, bank=bank, nb=nb, hcg1=hcg1: e.matmul(
                    bank[:, 0:nb], w1s[:, hcg1 % 2, fc, :], XINf[:, fc, 0:nb],
                    start=(fc == 0), stop=(fc == FC - 1)), waits=w)
            add("te", lambda e: None, inc="te")
            m_mm1 = cur("te")
            m_w1_cons[hcg1] = m_mm1
            add("sc", lambda e, hc=hc, bank=bank, nb=nb: e.activation(
                out=Hb[:, hc, 0:nb], in_=bank[:, 0:nb], func=ACT.Relu, bias=b1_s[:, hc:hc + 1]),
                waits=[("te", m_mm1)], inc="sc")
            m_relu[(b, hc)] = cur("sc")
        if debug and b == 0:
            add("sc", lambda e: e.dma_start(out=dbg_hb, in_=Hb[:, :, 0:128]),
                waits=[("sc", m_relu[(0, HC - 1)])], inc="dsc", n=16)
            m_dump3_dsc = cur("dsc")

        for hc in range(HC):
            w = [("te", m_w2_cons[hcg - 2])] if hcg >= 2 else [("dgp", W_W2)]
            add("sy", lambda e, hc=hc, hcg=hcg: e.dma_start(
                out=w2s[:, hcg % 2, :], in_=w2b[hc * 128:(hc + 1) * 128, :]),
                waits=w, inc="dsy", n=16)
            m_wl = cur("dsy")
            for sc_ in range(ntk):
                for dh in range(2):
                    w = []
                    if sc_ == 0 and dh == 0:
                        w = [("dsy", m_wl)]
                        if hc == 0:
                            w.append(("sc", m_relu[(b, HC - 1)]))
                            if b > 0:
                                w.append(("ve", m_mm2_done))
                    add("te", lambda e, hc=hc, sc_=sc_, dh=dh, hcg=hcg: e.matmul(
                        PS[sc_ * 2 + dh][:, :], Hb[:, hc, sc_ * 128:(sc_ + 1) * 128],
                        w2s[:, hcg % 2, dh * 512:(dh + 1) * 512],
                        start=(hc == 0), stop=(hc == HC - 1)), waits=w)
            add("te", lambda e: None, inc="te")
            m_w2_cons[hcg] = cur("te")
            hcg += 1
        m_mm2t = cur("te")

        for sc_ in range(ntk):
            r = r0 + sc_
            for dh in range(2):
                w = []
                if sc_ == 0 and dh == 0:
                    w = [("te", m_mm2t), ("dsy", R_LSB)]
                if dh == 0 and r >= 2:
                    w.append(("dsc", m_outdma.get(r - 2, 0)))
                add("ve", lambda e, sc_=sc_, dh=dh: e.tensor_tensor(
                    t32[:], PS[sc_ * 2 + dh][:, :], b2b_s[:, dh * 512:(dh + 1) * 512], op=ALU.add),
                    waits=w)
                add("ve", lambda e, sc_=sc_, dh=dh, r=r: e.tensor_scalar(
                    OUTt[:, r % 2, dh * 512:(dh + 1) * 512], t32[:], lsb[:, r, 1:2], None,
                    op0=ALU.mult))
            add("ve", lambda e: None, inc="ve")
            m_ep = cur("ve")
            add("sc", lambda e, r=r: e.dma_start(out=outm[r * 128:(r + 1) * 128, :],
                                                 in_=OUTt[:, r % 2, :]),
                waits=[("ve", m_ep)], inc="dsc", n=16)
            m_outdma[r] = cur("dsc")
        add("ve", lambda e: None, inc="ve")
        m_mm2_done = cur("ve")
    F_OUTD = cur("dsc")

    add("gp", lambda e: e.collective_compute(
        "AllGather", ALU.bypass, replica_groups=[list(range(NCORE))],
        ins=[outm.opt()], outs=[outall.opt()]),
        waits=[("dsc", F_OUTD)], inc="cc")
    CC_AGO = cur("cc")
    add("gp", lambda e: e.collective_compute(
        "AllReduce", ALU.add, replica_groups=[list(range(NCORE))],
        ins=[bar_in.opt()], outs=[bar3_out.opt()]),
        waits=[("cc", CC_AGO)], inc="cc")
    CC_BAR3 = cur("cc")
    add("gp", lambda e: e.collective_compute(
        "AllReduce", ALU.add, replica_groups=[list(range(NCORE))],
        ins=[bar_in.opt()], outs=[bar4_out.opt()]),
        waits=[("cc", CC_BAR3)], inc="cc")
    CC_OUT = cur("cc")
    MARK3 = {k: len(v) for k, v in SCHED.items()}

    ffn_st.close()

    bk_st = ExitStack()
    g0t = SB("g0t", [128, 2, D], BF16, bk_st)
    g1t = SB("g1t", [128, 2, D], BF16, bk_st)
    x1t = SB("x1t", [128, 2, D], F32, bk_st)
    acct = SB("acct", [128, D], F32, bk_st)
    ot = SB("ot", [128, 2, D], F32, bk_st)

    m_comb = {}
    m_odma = {}
    SEQ_TT = list(range(32)) + [0, 1]
    for si, tt in enumerate(SEQ_TT):
        w = [("cc", CC_OUT), ("ve", R_BACK)] if si == 0 else [("ve", m_comb[si - 2])] if si >= 2 else []
        add("gp", lambda e, tt=tt: e.indirect_dma_start(
            out=g0t[:, tt % 2, :], out_offset=None,
            in_=outall[0:1, :], in_offset=IndirectOffsetOnAxis(g0I[:, tt % 2, tt // 2:tt // 2 + 1], 0)),
            waits=w, inc="dgp", n=16)
        add("gp", lambda e, tt=tt: e.indirect_dma_start(
            out=g1t[:, tt % 2, :], out_offset=None,
            in_=outall[0:1, :], in_offset=IndirectOffsetOnAxis(g1I[:, tt % 2, tt // 2:tt // 2 + 1], 0)),
            inc="dgp", n=16)
        m_g = cur("dgp")
        w = [("dsc", A_DONE_DSC)] if si == 0 else [("ve", m_comb[si - 2])] if si >= 2 else []
        add("sy", lambda e, tt=tt: e.dma_start(out=x1t[:, tt % 2, :],
                                               in_=x1sp[tt * 128:(tt + 1) * 128, :]),
            waits=w, inc="dsy", n=16)
        m_xl = cur("dsy")
        w = [("dgp", m_g), ("dsy", m_xl)]
        if si >= 2:
            w.append(("dsc", m_odma[si - 2]))
        add("ve", lambda e, tt=tt: e.scalar_tensor_tensor(
            acct[:], g0t[:, tt % 2, :], myb[:, 2, tt % 2, tt // 2:tt // 2 + 1],
            x1t[:, tt % 2, :], op0=ALU.mult, op1=ALU.add), waits=w)
        add("ve", lambda e, tt=tt: e.scalar_tensor_tensor(
            ot[:, tt % 2, :], g1t[:, tt % 2, :], myb[:, 3, tt % 2, tt // 2:tt // 2 + 1],
            acct[:], op0=ALU.mult, op1=ALU.add), inc="ve")
        m_comb[si] = cur("ve")
        add("sc", lambda e, tt=tt: e.dma_start(out=out_t[tt * 128:(tt + 1) * 128, :],
                                               in_=ot[:, tt % 2, :]),
            waits=[("ve", m_comb[si])], inc="dsc", n=16)
        m_odma[si] = cur("dsc")
    B_OUT = cur("dsc")

    add("sc", lambda e: None, waits=[("dsc", B_OUT)])
    add("sy", lambda e: None, waits=[("dsc", B_OUT)])
    add("gp", lambda e: None, waits=[("dsc", B_OUT)])
    add("ve", lambda e: None, waits=[("dsc", B_OUT)])
    add("te", lambda e: None, waits=[("dsc", B_OUT)])

    bk_st.close()

    if limit < 4:
        mark = {1: MARK1, 2: MARK2, 3: MARK3}[limit]
        for k in SCHED:
            SCHED[k] = SCHED[k][:mark[k]]
        for s in CNT:
            CNT[s] = 0
        for k in SCHED:
            for seq, waits, inc, n, fn in SCHED[k]:
                if inc is not None:
                    CNT[inc] += n
    if os.environ.get("KSEQPRINT"):
        print("total steps:", SEQ[0])
    if limit < 4:
        tailw = [("dsc", A_DONE_DSC)]
        if limit >= 2:
            tailw.append(("ve", R_BACK))
            tailw.append(("dsy", R_LSB))
        if limit >= 3:
            tailw.append(("cc", CC_OUT))
        add("gp", lambda e: e.dma_start(out=out_t, in_=x1sp),
            waits=tailw, inc="dgp", n=16)
        TG = cur("dgp")
        add("sc", lambda e: None, waits=[("dgp", TG)])
        TEND = cur("dsc")
        add("sc", lambda e: None, waits=[("dsc", TEND)])
        add("sy", lambda e: None, waits=[("dsc", TEND)])
        add("gp", lambda e: None, waits=[("dsc", TEND)])

    if debug:
        DEND = cur("dsc")
        dbgw = [("dsc", DEND)]
        add("sc", lambda e: e.dma_start(out=dbg_x1, in_=x1sp), waits=dbgw, inc="dsc", n=16)
        add("gp", lambda e: e.dma_start(out=dbg_h2, in_=h2m), waits=dbgw, inc="dgp", n=16)
        if limit >= 2:
            add("sc", lambda e: e.dma_start(out=dbg_lg, in_=lgall), inc="dsc", n=16)
            add("sc", lambda e: e.dma_start(out=dbg_ls, in_=listd), inc="dsc", n=16)
            add("sc", lambda e: e.dma_start(out=dbg_rt[:, 0, :], in_=e1v[:]), inc="dsc", n=16)
            add("sc", lambda e: e.dma_start(out=dbg_rt[:, 1, :], in_=e2v[:]), inc="dsc", n=16)
            add("sc", lambda e: e.dma_start(out=dbg_rt[:, 2, :], in_=kpme[:]), inc="dsc", n=16)
            add("sc", lambda e: e.dma_start(out=dbg_rt[:, 3, :], in_=rkme[:]), inc="dsc", n=16)
            add("sc", lambda e: e.dma_start(out=dbg_rt[:, 4, :], in_=s1m[:]), inc="dsc", n=16)
            add("sc", lambda e: e.dma_start(out=dbg_rt[:, 5, :], in_=wvme[:]), inc="dsc", n=16)
            add("gp", lambda e: e.dma_start(out=dbg_rt[:, 6, :], in_=posi[:]), waits=dbgw, inc="dgp", n=16)
            add("sc", lambda e: e.dma_start(out=dbg_rt[:, 7, :], in_=rank_all[:, 0, :]), inc="dsc", n=16)
        if limit >= 3:
            add("gp", lambda e: e.dma_start(out=dbg_om, in_=outm), inc="dgp", n=16)
            add("gp", lambda e: e.dma_start(out=dbg_h2a, in_=h2all[0:8192, :]), inc="dgp", n=16)
        DEND2 = cur("dsc")
        DGND = cur("dgp")
        add("sc", lambda e: None, waits=[("dsc", DEND2), ("dgp", DGND)])

    sem_ctx = ExitStack()
    SEMS = {s: sem_ctx.enter_context(nc.semaphore(s)) for s in CNT}
    blk = ctx.enter_context(nc.Block())

    kcut = int(os.environ.get("KCUT", "0"))
    kept_dsc = 0
    if kcut:
        for seq, waits, inc, n, fn in SCHED["sc"]:
            if seq <= kcut and inc == "dsc":
                kept_dsc += n

    emit_cnt = {k: 0 for k in CNT}

    def mk(engname):
        def body(e):
            last_inst = None
            last_upd = None
            for seq, waits, inc, n, fn in SCHED[engname]:
                if kcut and seq > kcut:
                    continue
                for (s, v) in waits:
                    e.wait_ge(SEMS[s], v)
                inst = fn(e)
                if inst is not None:
                    last_inst = inst
                    last_upd = None
                if inc is not None:
                    emit_cnt[inc] += n
                    tgt = emit_cnt[inc]
                    if inst is not None:
                        inst.then_inc(SEMS[inc], n)
                        last_upd = (inc, tgt)
                    elif last_inst is not None and last_upd is None:
                        last_inst.then_inc(SEMS[inc], n)
                        last_upd = (inc, tgt)
                    elif last_upd is not None:
                        s0, v0 = last_upd
                        e.wait_ge(SEMS[s0], v0)
                        e.nop().then_inc(SEMS[inc], n)
                        last_upd = (inc, tgt)
                    else:
                        e.nop().then_inc(SEMS[inc], n)
            if kcut and engname == "sc":
                e.dma_start(out=out_t[0:512, :], in_=x1sp[0:512, :]).then_inc(SEMS["dsc"], 16)
                e.wait_ge(SEMS["dsc"], kept_dsc + 16)
        return body

    with sem_ctx:
        blk.sync(mk("sy"))
        blk.scalar(mk("sc"))
        blk.vector(mk("ve"))
        blk.tensor(mk("te"))
        blk.gpsimd(mk("gp"))
    ctx.close()
    return nc


_cache = {}


def kernel(L0_attn_x, L0_attn_Wk, L0_attn_Wv, L0_attn_Wo, L0_attn_freqs,
           L0_ffn_gate_w, L0_ffn_gate_b, L0_ffn_w1, L0_ffn_b1, L0_ffn_w2, L0_ffn_b2):
    x = np.asarray(L0_attn_x)
    Wv = np.asarray(L0_attn_Wv)
    Wo = np.asarray(L0_attn_Wo)
    freqs = np.asarray(L0_attn_freqs)
    gw = np.asarray(L0_ffn_gate_w)
    gb = np.asarray(L0_ffn_gate_b)
    w1 = np.asarray(L0_ffn_w1)
    b1 = np.asarray(L0_ffn_b1)
    w2 = np.asarray(L0_ffn_w2)
    b2 = np.asarray(L0_ffn_b2)

    limit = int(os.environ.get("KLIMIT", "4"))
    debug = int(os.environ.get("KDEBUG", "0"))
    key = (limit, debug)
    if key not in _cache:
        _cache[key] = build(limit, debug)
    nc = _cache[key]

    wvT = np.ascontiguousarray(Wv.T.reshape(FC, 128, D).transpose(1, 0, 2))
    woT = np.ascontiguousarray(Wo.T.reshape(FC, 128, D).transpose(1, 0, 2))
    gwl = np.ascontiguousarray(gw.reshape(FC, 128, E).transpose(1, 0, 2))
    frl = np.ascontiguousarray(freqs.reshape(FC, 128).T)
    gbl = np.ascontiguousarray(gb.reshape(E, 1))

    in_maps = []
    for c in range(NCORE):
        psel = np.zeros((128, 16), np.float32)
        psel[16 * c + np.arange(16), np.arange(16)] = 1.0
        in_maps.append({
            "xT": np.ascontiguousarray(x[c].T.reshape(FC, 128, T).transpose(1, 0, 2)),
            "wvT": wvT, "woT": woT, "gw": gwl, "gb": gbl, "fr": frl,
            "cid": np.full((128, 1), float(c), np.float32),
            "psel": psel,
            "w1": np.ascontiguousarray(w1[c]),
            "b1": np.ascontiguousarray(b1[c].reshape(HC, 128).T),
            "w2": np.ascontiguousarray(w2[c]),
            "b2": np.ascontiguousarray(b2[c].reshape(1, D)),
            "zin": np.zeros((512, D), np.float32),
        })
    res = run_bass_kernel_spmd(nc, in_maps, core_ids=list(range(NCORE)))
    out = np.stack([np.asarray(res.results[c]["yout"]) for c in range(NCORE)], axis=0)
    if debug:
        kernel._dbg = res.results
    return out


# version 04 (final)
# speedup vs baseline: 1.0155x; 1.0155x over previous
import os
import numpy as np
import concourse.bass as bass
import concourse.mybir as mybir
from concourse.bass_utils import run_bass_kernel_spmd
from concourse.bass import IndirectOffsetOnAxis
from contextlib import ExitStack

F32 = mybir.dt.float32
BF16 = mybir.dt.bfloat16
I32 = mybir.dt.int32
ALU = mybir.AluOpType
ACT = mybir.ActivationFunctionType
AX = mybir.AxisListType.X

NCORE = 8
T = 4096
D = 1024
FC = 8
NCH = 8
CHT = 512
E = 8
FF = 4096
HC = 32
CAP = 5120
CPAD = 5248
NTILE = 41
STOT = 32768
TWO_PI = 6.283185307179586
EPS = 1e-5
MAGIC = float(2.0 ** 23)
MAGIC2 = float(1.5 * 2.0 ** 23)
PI = float(np.pi)


def _trunc_bits(x, bits):
    m, e = np.frexp(np.float64(x))
    return float(np.ldexp(np.floor(np.ldexp(m, bits)), int(e - bits)))


CW1 = _trunc_bits(TWO_PI, 11)
CW2 = _trunc_bits(TWO_PI - CW1, 11)
CW3 = float(np.float64(TWO_PI) - CW1 - CW2)

BLOCKS = [(b * 4, 4) for b in range(10)] + [(40, 1)]


def build(limit=4, debug=0):
    nc = bass.Bass("TRN2", target_bir_lowering=False, debug=False)

    xT_in = nc.dram_tensor("xT", [128, FC, T], F32, kind="ExternalInput").ap()
    wvT_in = nc.dram_tensor("wvT", [128, FC, D], F32, kind="ExternalInput").ap()
    woT_in = nc.dram_tensor("woT", [128, FC, D], F32, kind="ExternalInput").ap()
    gw_in = nc.dram_tensor("gw", [128, FC, E], F32, kind="ExternalInput").ap()
    gb_in = nc.dram_tensor("gb", [E, 1], F32, kind="ExternalInput").ap()
    fr_in = nc.dram_tensor("fr", [128, FC], F32, kind="ExternalInput").ap()
    cid_in = nc.dram_tensor("cid", [128, 1], F32, kind="ExternalInput").ap()
    psel_in = nc.dram_tensor("psel", [128, 16], F32, kind="ExternalInput").ap()
    w1_in = nc.dram_tensor("w1", [D, FF], F32, kind="ExternalInput").ap()
    b1_in = nc.dram_tensor("b1", [128, HC], F32, kind="ExternalInput").ap()
    w2_in = nc.dram_tensor("w2", [FF, D], F32, kind="ExternalInput").ap()
    b2_in = nc.dram_tensor("b2", [1, D], F32, kind="ExternalInput").ap()
    zin_in = nc.dram_tensor("zin", [512, D], F32, kind="ExternalInput").ap()
    out_t = nc.dram_tensor("yout", [T, D], F32, kind="ExternalOutput").ap()
    if debug:
        dbg_x1 = nc.dram_tensor("dbg_x1", [T, D], F32, kind="ExternalOutput").ap()
        dbg_h2 = nc.dram_tensor("dbg_h2", [T, D], F32, kind="ExternalOutput").ap()
        dbg_lg = nc.dram_tensor("dbg_lg", [STOT, E], F32, kind="ExternalOutput").ap()
        dbg_ls = nc.dram_tensor("dbg_ls", [CPAD + 2, 2], F32, kind="ExternalOutput").ap()
        dbg_om = nc.dram_tensor("dbg_om", [CPAD, D], F32, kind="ExternalOutput").ap()
        dbg_rt = nc.dram_tensor("dbg_rt", [128, 8, 256], F32, kind="ExternalOutput").ap()
        dbg_h2a = nc.dram_tensor("dbg_h2a", [8192, D], BF16, kind="ExternalOutput").ap()
        dbg_xg = nc.dram_tensor("dbg_xg", [128, 2, D], BF16, kind="ExternalOutput").ap()
        dbg_gi = nc.dram_tensor("dbg_gi", [128, NTILE], I32, kind="ExternalOutput").ap()
        dbg_xi = nc.dram_tensor("dbg_xi", [128, 8, 128], BF16, kind="ExternalOutput").ap()
        dbg_hb = nc.dram_tensor("dbg_hb", [128, 32, 128], BF16, kind="ExternalOutput").ap()

    x1sp = nc.dram_tensor("x1sp", [T, D], F32).ap()
    h2m = nc.dram_tensor("h2m", [T, D], BF16).ap()
    h2all = nc.dram_tensor("h2all", [STOT, D], BF16).ap()
    lgm = nc.dram_tensor("lgm", [T, E], F32).ap()
    lgall = nc.dram_tensor("lgall", [STOT, E], F32).ap()
    w2b = nc.dram_tensor("w2b", [128, HC, D], BF16).ap()
    w1b = nc.dram_tensor("w1b", [HC, 128, FC, 128], BF16).ap()
    listd = nc.dram_tensor("listd", [CPAD + 2, 2], F32).ap()
    bar_in = nc.dram_tensor("bar_in", [8, 4], F32).ap()
    bar_out = nc.dram_tensor("bar_out", [8, 4], F32).ap()
    bar2_out = nc.dram_tensor("bar2_out", [8, 4], F32).ap()
    bar3_out = nc.dram_tensor("bar3_out", [8, 4], F32).ap()
    bar4_out = nc.dram_tensor("bar4_out", [8, 4], F32).ap()
    settle_d = nc.dram_tensor("settle_d", [128, D], F32).ap()
    zrowB = nc.dram_tensor("zrowB", [4096, D], BF16).ap()
    outm = nc.dram_tensor("outm", [CPAD, D], BF16).ap()
    outall = nc.dram_tensor("outall", [NCORE * CPAD, D], BF16).ap()

    ctx = ExitStack()

    def SB(name, shape, dt, st=None):
        return (st or ctx).enter_context(nc.sbuf_tensor(name, shape, dt))

    wv_b = SB("wv_b", [128, FC, D], BF16)
    wo_b = SB("wo_b", [128, FC, D], BF16)
    gw_f = SB("gw_f", [128, FC, E], F32)
    onecolF = SB("onecolF", [128, 1], F32)
    gsum8 = SB("gsum8", [E, 1], F32)
    fr_s = SB("fr_s", [128, FC], F32)
    f2p_s = SB("f2p_s", [128, FC], F32)
    gb_s = SB("gb_s", [E, 1], F32)
    cid_s = SB("cid_s", [128, 1], F32)
    psel_s = SB("psel_s", [128, 16], F32)
    b1_s = SB("b1_s", [128, HC], F32)
    b2r_s = SB("b2r_s", [1, D], F32)
    csb_c = SB("csb_c", [128, FC, 256], F32)
    csb_s = SB("csb_s", [128, FC, 256], F32)
    csC = SB("csC", [128, FC, 16], F32)
    csS = SB("csS", [128, FC, 16], F32)
    csSn = SB("csSn", [128, FC, 16], F32)
    carr_c = SB("carr_c", [128, FC], F32)
    carr_s = SB("carr_s", [128, FC], F32)
    ones513 = SB("ones513", [128, 513], F32)
    onecol_b = SB("onecol_b", [128, 1], BF16)
    ones1f = SB("ones1f", [1, 128], F32)
    onesq = SB("onesq", [128, 128], F32)
    identf = SB("identf", [128, 128], F32)
    identb = SB("identb", [128, 128], BF16)
    l128b = SB("l128b", [128, 128], BF16)
    iotaR = SB("iotaR", [128, 128], F32)
    iotaC = SB("iotaC", [128, 128], F32)
    iotaU = SB("iotaU", [128, 256], F32)
    iota256 = SB("iota256", [128, 256], F32)
    iotaB16 = SB("iotaB16", [128, 16], F32)
    zerS = SB("zerS", [128, 96], F32)
    lsb = SB("lsb", [128, NTILE, 2], F32)
    gidxI = SB("gidxI", [128, NTILE], I32)
    myb = SB("myb", [128, 4, 2, 16], F32)
    g0I = SB("g0I", [128, 2, 16], I32)
    g1I = SB("g1I", [128, 2, 16], I32)
    b2b_s = SB("b2b_s", [128, D], F32)

    PS = [ctx.enter_context(nc.psum_tensor(f"ps{k}", [128, 512], F32)) for k in range(8)]

    SCHED = {"sy": [], "sc": [], "ve": [], "te": [], "gp": []}
    CNT = {k: 0 for k in ["dsy", "dsc", "dgp", "sc", "ve", "te", "gp", "cc"]}
    SEQ = [0]

    def add(eng, fn, waits=(), inc=None, n=1):
        if inc is not None:
            CNT[inc] += n
        SEQ[0] += 1
        SCHED[eng].append((SEQ[0], tuple((s, v) for (s, v) in waits if v > 0), inc, n, fn))

    def cur(s):
        return CNT[s]

    add("gp", lambda e: e.iota(iotaU[:], pattern=[[1, 256]], base=0,
                               channel_multiplier=256, allow_small_or_imprecise_dtypes=True))
    add("gp", lambda e: e.iota(iota256[:], pattern=[[1, 256]], base=0,
                               channel_multiplier=0, allow_small_or_imprecise_dtypes=True))
    add("gp", lambda e: e.iota(iotaB16[:], pattern=[[256, 16]], base=0,
                               channel_multiplier=0, allow_small_or_imprecise_dtypes=True),
        inc="gp")
    W_IOTA = cur("gp")

    add("ve", lambda e: e.memset(ones513[:], 1.0))
    add("ve", lambda e: e.memset(onecol_b[:], 1.0))
    add("ve", lambda e: e.memset(onecolF[:], 1.0))
    add("ve", lambda e: e.memset(ones1f[:], 1.0))
    add("ve", lambda e: e.memset(onesq[:], 1.0))
    add("ve", lambda e: e.memset(zerS[:], 0.0))
    add("ve", lambda e: e.memset(carr_c[:], 0.0))
    add("ve", lambda e: e.memset(carr_s[:], 0.0), inc="ve")
    W_MEMS = cur("ve")

    add("gp", lambda e: e.affine_select(identf[:], onesq[:], pattern=[[-1, 128]],
                                        compare_op=ALU.is_equal, fill=0.0, channel_multiplier=1),
        waits=[("ve", W_MEMS)])
    add("gp", lambda e: e.affine_select(identb[:], onesq[:], pattern=[[-1, 128]],
                                        compare_op=ALU.is_equal, fill=0.0, channel_multiplier=1))
    add("gp", lambda e: e.iota(iotaR[:], pattern=[[0, 128]], base=0,
                               channel_multiplier=1, allow_small_or_imprecise_dtypes=True))
    add("gp", lambda e: e.iota(iotaC[:], pattern=[[1, 128]], base=0,
                               channel_multiplier=0, allow_small_or_imprecise_dtypes=True),
        inc="gp")
    W_CONST = cur("gp")
    add("ve", lambda e: e.tensor_tensor(l128b[:], iotaR[:], iotaC[:], op=ALU.is_lt),
        waits=[("gp", W_CONST)], inc="ve")
    W_L128 = cur("ve")

    add("sy", lambda e: e.dma_start(out=fr_s[:], in_=fr_in), inc="dsy", n=16)
    add("sy", lambda e: e.dma_start(out=gb_s[:], in_=gb_in), inc="dsy", n=16)
    add("sy", lambda e: e.dma_start(out=cid_s[:], in_=cid_in), inc="dsy", n=16)
    add("sy", lambda e: e.dma_start(out=psel_s[:], in_=psel_in), inc="dsy", n=16)
    add("sy", lambda e: e.dma_start(out=b1_s[:], in_=b1_in), inc="dsy", n=16)
    add("sy", lambda e: e.dma_start(out=b2r_s[:], in_=b2_in), inc="dsy", n=16)
    W_SMALL = cur("dsy")

    add("gp", lambda e: e.dma_start(out=wv_b[:], in_=wvT_in), inc="dgp", n=16)
    add("gp", lambda e: e.dma_start(out=wo_b[:], in_=woT_in), inc="dgp", n=16)
    W_WATT = cur("dgp")
    add("sy", lambda e: e.dma_start(out=gw_f[:], in_=gw_in), inc="dsy", n=16)

    add("gp", lambda e: e.dma_start(out=zrowB[0:512, :], in_=zin_in), inc="dgp", n=16)
    Z0 = cur("dgp")
    add("sc", lambda e: e.dma_start(out=zrowB[512:1024, :], in_=zrowB[0:512, :]),
        waits=[("dgp", Z0)], inc="dsc", n=16)
    add("sc", lambda e: e.dma_start(out=zrowB[1024:2048, :], in_=zrowB[0:1024, :]),
        waits=[("dsc", cur("dsc"))], inc="dsc", n=16)
    add("sc", lambda e: e.dma_start(out=zrowB[2048:4096, :], in_=zrowB[0:2048, :]),
        waits=[("dsc", cur("dsc"))], inc="dsc", n=16)
    ZF = cur("dsc")
    for zi in range(8):
        add("sc", lambda e, zi=zi: e.dma_start(
            out=h2all[4096 * zi:4096 * (zi + 1), :], in_=zrowB),
            waits=[("dsc", ZF)], inc="dsc", n=16)
    for zi in range(10):
        add("sc", lambda e, zi=zi: e.dma_start(
            out=outall[4096 * zi:4096 * (zi + 1), :], in_=zrowB),
            waits=[("dsc", ZF)], inc="dsc", n=16)
    add("sc", lambda e: e.dma_start(out=outall[40960:41984, :], in_=zrowB[0:1024, :]),
        waits=[("dsc", ZF)], inc="dsc", n=16)
    add("sc", lambda e: e.dma_start(
        out=lgall.rearrange("(a b) e -> a (b e)", b=128), in_=zin_in[0:256, :]),
        inc="dsc", n=16)
    W_GWF = cur("dsy")
    W_W1 = cur("dgp")

    attn_st = ExitStack()
    xcb = SB("xcb", [128, FC, CHT], F32, attn_st)
    xbb = SB("xbb", [128, FC, CHT], BF16, attn_st)
    xnb = SB("xnb", [128, FC, CHT], BF16, attn_st)
    rowscr = SB("rowscr", [1, 4, CHT], F32, attn_st)
    tscr = SB("tscr", [128, CHT], F32, attn_st)
    c_f = SB("c_f", [128, CHT], F32, attn_st)
    s_f = SB("s_f", [128, CHT], F32, attn_st)
    tre1 = SB("tre1", [128, CHT], F32, attn_st)
    tre2 = SB("tre2", [128, CHT], F32, attn_st)
    trec1 = SB("trec1", [128, CHT], F32, attn_st)
    trec2 = SB("trec2", [128, CHT], F32, attn_st)
    ain = SB("ain", [128, 513], F32, attn_st)
    Aout = SB("Aout", [128, 513], F32, attn_st)
    Bout = SB("Bout", [128, 513], F32, attn_st)
    rb = SB("rb", [128, FC, CHT], BF16, attn_st)
    x1b = SB("x1b", [128, FC, CHT], F32, attn_st)
    lgrow = SB("lgrow", [E, CHT], F32, attn_st)
    lgr8 = SB("lgr8", [E, 2, CHT], F32, attn_st)
    lgtt = SB("lgtt", [128, 32], F32, attn_st)
    evrow = SB("evrow", [128, 1024], F32, attn_st)
    evrowb = SB("evrowb", [128, 1024], BF16, attn_st)
    rotscr2 = SB("rotscr2", [128, FC, 16], F32, attn_st)
    rotk2 = SB("rotk2", [128, FC, 16], F32, attn_st)
    rotscr = xcb[:, :, 0:256]
    rotk = xcb[:, :, 256:512]
    w2st = xbb[:].rearrange("p a b -> p (a b)").rearrange("p (a d) -> p a d", a=4)

    w1st = xnb[:].rearrange("p a b -> p (a b)")[:, 0:1024].rearrange("p (a b) -> p a b", a=FC)
    prev = W_W1
    for hc in range(HC):
        w = [("dgp", prev)]
        add("gp", lambda e, hc=hc: e.dma_start(
            out=w1st, in_=w1_in[:, hc * 128:(hc + 1) * 128].rearrange("(a p) h -> p a h", p=128)),
            waits=w, inc="dgp", n=16)
        m_l = cur("dgp")
        add("gp", lambda e, hc=hc: e.dma_start(out=w1b[hc], in_=w1st),
            waits=[("dgp", m_l)], inc="dgp", n=16)
        prev = cur("dgp")
    W_W1B = cur("dgp")

    prev = W_W1B
    for j in range(8):
        w = [("dgp", prev)]
        add("gp", lambda e, j=j: e.dma_start(
            out=w2st, in_=w2_in[j * 512:(j + 1) * 512, :].rearrange("(a p) d -> p a d", p=128)),
            waits=w, inc="dgp", n=16)
        m_l = cur("dgp")
        add("gp", lambda e, j=j: e.dma_start(
            out=w2b[:, j * 4:(j + 1) * 4, :], in_=w2st),
            waits=[("dgp", m_l)], inc="dgp", n=16)
        prev = cur("dgp")
    W_W2 = cur("dgp")

    add("ve", lambda e: e.tensor_scalar(f2p_s[:], fr_s[:], 1.0 / TWO_PI, None, op0=ALU.mult),
        waits=[("dsy", W_SMALL)], inc="ve")
    W_F2P = cur("ve")

    def rotor_tables(iot, n, scr, kb, outc, outs):
        for fc in range(FC):
            add("sc", lambda e, fc=fc: e.activation(out=scr[:, fc, :], in_=iot[:, 0:n],
                                                    func=ACT.Copy, scale=f2p_s[:, fc:fc + 1]),
                waits=[("ve", W_F2P), ("gp", W_IOTA)] if fc == 0 else [])
        add("sc", lambda e: None, inc="sc")
        m0 = cur("sc")
        add("ve", lambda e: e.tensor_scalar(kb[:], scr[:], MAGIC, -MAGIC,
                                            op0=ALU.add, op1=ALU.add),
            waits=[("sc", m0)], inc="ve")
        mk = cur("ve")
        for fc in range(FC):
            add("sc", lambda e, fc=fc: e.activation(out=scr[:, fc, :], in_=iot[:, 0:n],
                                                    func=ACT.Copy, scale=fr_s[:, fc:fc + 1]),
                waits=[("ve", mk)] if fc == 0 else [])
        add("sc", lambda e: None, inc="sc")
        m1 = cur("sc")
        add("ve", lambda e: e.scalar_tensor_tensor(scr[:], kb[:], -CW1, scr[:],
                                                   op0=ALU.mult, op1=ALU.add),
            waits=[("sc", m1)])
        add("ve", lambda e: e.scalar_tensor_tensor(scr[:], kb[:], -CW2, scr[:],
                                                   op0=ALU.mult, op1=ALU.add))
        add("ve", lambda e: e.scalar_tensor_tensor(scr[:], kb[:], -CW3, scr[:],
                                                   op0=ALU.mult, op1=ALU.add))
        add("ve", lambda e: e.tensor_scalar(kb[:], scr[:], PI, None, op0=ALU.min))
        add("ve", lambda e: e.tensor_scalar(kb[:], kb[:], -PI, None, op0=ALU.max), inc="ve")
        m2 = cur("ve")
        add("sc", lambda e: e.activation(out=outs[:], in_=kb[:], func=ACT.Sin),
            waits=[("ve", m2)], inc="sc")
        m3 = cur("sc")
        add("ve", lambda e: e.tensor_scalar(kb[:], scr[:], PI / 2, None, op0=ALU.add),
            waits=[("sc", m3)])
        add("ve", lambda e: e.tensor_scalar(scr[:], kb[:], PI, None, op0=ALU.is_gt))
        add("ve", lambda e: e.scalar_tensor_tensor(kb[:], scr[:], -TWO_PI, kb[:],
                                                   op0=ALU.mult, op1=ALU.add))
        add("ve", lambda e: e.tensor_scalar(kb[:], kb[:], PI, None, op0=ALU.min))
        add("ve", lambda e: e.tensor_scalar(kb[:], kb[:], -PI, None, op0=ALU.max), inc="ve")
        m4 = cur("ve")
        add("sc", lambda e: e.activation(out=outc[:], in_=kb[:], func=ACT.Sin),
            waits=[("ve", m4)], inc="sc")

    rotor_tables(iota256, 256, rotscr.rearrange("p a b -> p a b"), rotk.rearrange("p a b -> p a b"),
                 csb_c, csb_s)
    rotor_tables(iotaB16, 16, rotscr2, rotk2, csC, csS)
    add("ve", lambda e: e.tensor_scalar(csSn[:], csS[:], -1.0, None, op0=ALU.mult),
        waits=[("sc", cur("sc"))], inc="ve")
    W_ROT = cur("ve")
    W_ROTSC = cur("sc")

    add("te", lambda e: e.matmul(PS[6][:, :], ones1f[:, :], b2r_s[0:1, 0:512],
                                 start=True, stop=True),
        waits=[("dsy", W_SMALL), ("ve", W_MEMS)])
    add("te", lambda e: e.matmul(PS[7][:, :], ones1f[:, :], b2r_s[0:1, 512:1024],
                                 start=True, stop=True), inc="te")
    W_B2MM = cur("te")
    add("ve", lambda e: e.tensor_copy(b2b_s[:, 0:512], PS[6][:, :]), waits=[("te", W_B2MM)])
    add("ve", lambda e: e.tensor_copy(b2b_s[:, 512:1024], PS[7][:, :]), inc="ve")
    W_B2B = cur("ve")
    for fc in range(FC):
        add("te", lambda e, fc=fc: e.matmul(PS[0][0:8, 0:1], gw_f[:, fc, :], onecolF[:, :],
                                            start=(fc == 0), stop=(fc == FC - 1)),
            waits=[("dsy", W_GWF), ("ve", W_B2B)] if fc == 0 else [])
    add("te", lambda e: None, inc="te")
    W_GSMM = cur("te")
    add("ve", lambda e: e.tensor_copy(gsum8[:], PS[0][0:8, 0:1]), waits=[("te", W_GSMM)], inc="ve")
    W_GSUM = cur("ve")

    prev_x_free = W_ROT
    prev_xbb_free = ("dgp", W_W2)
    prev_ps45_sc = 0
    prev_ps5_ve = 0
    prev_lg_dma = 0
    prev_spill_dma = 0
    prev_h2_dma = 0

    def ln_stats(xf32, xb16, sqb16, wv, wt):
        add("sc", lambda e: e.activation(out=xb16[:], in_=xf32[:], func=ACT.Copy), waits=wv)
        add("sc", lambda e: e.activation(out=sqb16[:], in_=xf32[:], func=ACT.Square), inc="sc")
        m0 = cur("sc")
        for fc in range(FC):
            add("te", lambda e, fc=fc: e.matmul(PS[4][0:1, :], onecol_b[:, :], xb16[:, fc, :],
                                                start=(fc == 0), stop=(fc == FC - 1)),
                waits=([("sc", m0)] + wt) if fc == 0 else [])
        for fc in range(FC):
            add("te", lambda e, fc=fc: e.matmul(PS[5][0:1, :], onecol_b[:, :], sqb16[:, fc, :],
                                                start=(fc == 0), stop=(fc == FC - 1)))
        add("te", lambda e: None, inc="te")
        m1 = cur("te")
        add("ve", lambda e: e.tensor_scalar(rowscr[0:1, 0, :], PS[4][0:1, :], 1.0 / D, None, op0=ALU.mult),
            waits=[("te", m1)])
        add("ve", lambda e: e.tensor_scalar(rowscr[0:1, 1, :], PS[5][0:1, :], 1.0 / D, None, op0=ALU.mult))
        add("ve", lambda e: e.tensor_tensor(rowscr[0:1, 2, :], rowscr[0:1, 0, :], rowscr[0:1, 0, :],
                                            op=ALU.mult))
        add("ve", lambda e: e.tensor_tensor(rowscr[0:1, 1, :], rowscr[0:1, 1, :], rowscr[0:1, 2, :],
                                            op=ALU.subtract))
        add("ve", lambda e: e.tensor_scalar(rowscr[0:1, 1, :], rowscr[0:1, 1, :], EPS, None, op0=ALU.add),
            inc="ve")
        m2 = cur("ve")
        add("sc", lambda e: e.activation(out=rowscr[0:1, 2, :], in_=rowscr[0:1, 1, :], func=ACT.Ln),
            waits=[("ve", m2)])
        add("sc", lambda e: e.activation(out=rowscr[0:1, 3, :], in_=rowscr[0:1, 2, :], func=ACT.Exp,
                                         scale=-0.5), inc="sc")
        m3 = cur("sc")
        add("ve", lambda e: e.scalar_tensor_tensor(rowscr[0:1, 1, :], rowscr[0:1, 0, :], -1.0,
                                                   rowscr[0:1, 3, :], op0=ALU.mult, op1=ALU.mult),
            waits=[("sc", m3)], inc="ve")
        m4 = cur("ve")
        add("te", lambda e: e.matmul(PS[6][:, :], ones1f[:, :], rowscr[0:1, 3, :], start=True, stop=True),
            waits=[("ve", m4)])
        add("te", lambda e: e.matmul(PS[7][:, :], ones1f[:, :], rowscr[0:1, 1, :], start=True, stop=True),
            inc="te")
        return cur("te")

    for i in range(NCH):
        add("sy", lambda e, i=i: e.dma_start(out=xcb[:], in_=xT_in[:, :, i * CHT:(i + 1) * CHT]),
            waits=[("ve", prev_x_free)], inc="dsy", n=16)
        m_x = cur("dsy")

        wv = [("dsy", m_x), ("ve", W_B2B)]
        if i == 0:
            wv.append(prev_xbb_free)
        wt = [("te", W_B2MM)]
        if prev_ps45_sc:
            wt = [("sc", prev_ps45_sc), ("ve", prev_ps5_ve)]
        m_bc1 = ln_stats(xcb, xbb, xnb, wv, wt)
        for fc in range(FC):
            add("ve", lambda e, fc=fc: e.tensor_tensor(tscr[:], xcb[:, fc, :], PS[6][:, :], op=ALU.mult),
                waits=[("te", m_bc1)] if fc == 0 else [])
            add("ve", lambda e, fc=fc: e.tensor_tensor(xnb[:, fc, :], tscr[:], PS[7][:, :], op=ALU.add))
        add("ve", lambda e: None, inc="ve")
        m_xn = cur("ve")

        m_cf_prev = 0
        m_chains = 0
        for g in range(2):
            bank_ids = list(range(0, 4)) if g == 0 else list(range(4, 8))
            for k, dc in enumerate(range(g * 4, g * 4 + 4)):
                for fc in range(FC):
                    w = [("ve", m_xn)] if (k == 0 and fc == 0) else []
                    add("te", lambda e, dc=dc, fc=fc, b=bank_ids[k]: e.matmul(
                        PS[b][:, :], wv_b[:, fc, dc * 128:(dc + 1) * 128], xnb[:, fc, :],
                        start=(fc == 0), stop=(fc == FC - 1)), waits=w)
            add("te", lambda e: None, inc="te")
            m_group = cur("te")

            for k, dc in enumerate(range(g * 4, g * 4 + 4)):
                b = bank_ids[k]
                for h in range(2):
                    bb = 2 * i + h
                    sl = slice(h * 256, (h + 1) * 256)
                    w = []
                    if m_cf_prev:
                        w = [("ve", m_cf_prev)]
                    elif i == 0 and dc == 0 and h == 0:
                        w = [("ve", W_ROT)]
                    add("sc", lambda e, dc=dc, bb=bb, sl=sl: e.activation(
                        out=trec1[:, sl], in_=csb_c[:, dc, :], func=ACT.Copy,
                        scale=csC[:, dc, bb:bb + 1]), waits=w if h == 0 else [])
                    add("sc", lambda e, dc=dc, bb=bb, sl=sl: e.activation(
                        out=trec2[:, sl], in_=csb_s[:, dc, :], func=ACT.Copy,
                        scale=csC[:, dc, bb:bb + 1]))
                add("sc", lambda e: None, inc="sc")
                m_rec = cur("sc")
                for h in range(2):
                    bb = 2 * i + h
                    sl = slice(h * 256, (h + 1) * 256)
                    add("ve", lambda e, dc=dc, bb=bb, sl=sl: e.scalar_tensor_tensor(
                        c_f[:, sl], csb_s[:, dc, :], csSn[:, dc, bb:bb + 1], trec1[:, sl],
                        op0=ALU.mult, op1=ALU.add),
                        waits=[("sc", m_rec)] if h == 0 else [])
                    add("ve", lambda e, dc=dc, bb=bb, sl=sl: e.scalar_tensor_tensor(
                        s_f[:, sl], csb_c[:, dc, :], csS[:, dc, bb:bb + 1], trec2[:, sl],
                        op0=ALU.mult, op1=ALU.add))
                add("ve", lambda e: None, inc="ve")
                m_cf_prev = cur("ve")

                add("ve", lambda e, dc=dc: e.tensor_copy(ain[:, 0:1], carr_c[:, dc:dc + 1]),
                    waits=[("te", m_group)])
                add("ve", lambda e, b=b: e.tensor_tensor(ain[:, 1:513], PS[b][:, :], c_f[:, :],
                                                         op=ALU.mult))
                add("ve", lambda e: e.tensor_tensor_scan(Aout[:, :], ones513[:, :], ain[:, :], 0.0,
                                                         op0=ALU.mult, op1=ALU.add), inc="ve")
                m_sa = cur("ve")
                add("ve", lambda e, dc=dc: e.tensor_copy(carr_c[:, dc:dc + 1], Aout[:, 512:513]),
                    waits=[("ve", m_sa)])
                add("ve", lambda e, dc=dc: e.tensor_copy(ain[:, 0:1], carr_s[:, dc:dc + 1]))
                add("ve", lambda e, b=b: e.tensor_tensor(ain[:, 1:513], PS[b][:, :], s_f[:, :],
                                                         op=ALU.mult))
                add("ve", lambda e: e.tensor_tensor_scan(Bout[:, :], ones513[:, :], ain[:, :], 0.0,
                                                         op0=ALU.mult, op1=ALU.add), inc="ve")
                m_sb = cur("ve")
                add("ve", lambda e, dc=dc: e.tensor_copy(carr_s[:, dc:dc + 1], Bout[:, 512:513]),
                    waits=[("ve", m_sb)])
                add("ve", lambda e: e.tensor_tensor(tre1[:, :], c_f[:, :], Aout[:, 1:513], op=ALU.mult))
                add("ve", lambda e: e.tensor_tensor(tre2[:, :], s_f[:, :], Bout[:, 1:513], op=ALU.mult))
                add("ve", lambda e, dc=dc: e.tensor_tensor(rb[:, dc, :], tre1[:, :], tre2[:, :],
                                                           op=ALU.add), inc="ve")
            m_chains = cur("ve")

        for g in range(2):
            bank_ids = list(range(0, 4)) if g == 0 else list(range(4, 8))
            for k, dc in enumerate(bank_ids):
                for kc in range(FC):
                    w = [("ve", m_chains)] if (g == 0 and k == 0 and kc == 0) else []
                    add("te", lambda e, dc=dc, kc=kc: e.matmul(
                        PS[dc][:, :], wo_b[:, kc, dc * 128:(dc + 1) * 128], rb[:, kc, :],
                        start=(kc == 0), stop=(kc == FC - 1)), waits=w)
        add("te", lambda e: None, inc="te")
        m_omm = cur("te")
        for dc in range(FC):
            add("ve", lambda e, dc=dc: e.tensor_tensor(x1b[:, dc, :], xcb[:, dc, :], PS[dc][:, :],
                                                       op=ALU.add),
                waits=[("te", m_omm)] if dc == 0 else [])
        add("ve", lambda e: None, inc="ve")
        m_x1 = cur("ve")
        prev_x_free = m_x1

        m_bc2 = ln_stats(x1b, xbb, xnb, [("ve", m_x1)], [("ve", m_x1)])
        for fc in range(FC):
            add("ve", lambda e, fc=fc: e.tensor_tensor(tscr[:], x1b[:, fc, :], PS[6][:, :], op=ALU.mult),
                waits=[("te", m_bc2)] if fc == 0 else [])
            add("ve", lambda e, fc=fc: e.tensor_tensor(xnb[:, fc, :], tscr[:], PS[7][:, :], op=ALU.add))
        add("ve", lambda e: None, inc="ve")
        m_h2 = cur("ve")

        for fc in range(FC):
            add("te", lambda e, fc=fc: e.matmul(PS[4][0:8, :], gw_f[:, fc, :], x1b[:, fc, :],
                                                start=(fc == 0), stop=(fc == FC - 1)),
                waits=[("ve", m_h2), ("te", W_GSMM)] if fc == 0 else [])
        add("te", lambda e: e.matmul(PS[5][0:8, :], ones1f[0:1, 0:8], rowscr[0:1, 3, :],
                                     start=True, stop=True))
        add("te", lambda e: e.matmul(PS[6][0:8, :], ones1f[0:1, 0:8], rowscr[0:1, 1, :],
                                     start=True, stop=True))
        add("te", lambda e: None, inc="te")
        m_lg = cur("te")
        add("sc", lambda e: e.activation(out=lgrow[:, :], in_=PS[4][0:8, :], func=ACT.Copy),
            waits=[("te", m_lg)], inc="sc")
        m_lgr = cur("sc")
        prev_ps45_sc = m_lgr
        add("ve", lambda e: e.tensor_copy(lgr8[:, 0, :], PS[5][0:8, :]), waits=[("sc", m_lgr)])
        add("ve", lambda e: e.tensor_copy(lgr8[:, 1, :], PS[6][0:8, :]), inc="ve")
        m_lg8 = cur("ve")
        add("ve", lambda e: e.tensor_tensor(lgrow[:, :], lgrow[:, :], lgr8[:, 0, :], op=ALU.mult),
            waits=[("ve", m_lg8)])
        add("ve", lambda e: e.scalar_tensor_tensor(lgrow[:, :], lgr8[:, 1, :], gsum8[0:8, 0:1],
                                                   lgrow[:, :], op0=ALU.mult, op1=ALU.add))
        add("ve", lambda e: e.tensor_scalar(lgrow[:, :], lgrow[:, :], gb_s[0:8, 0:1], None,
                                            op0=ALU.add), inc="ve")
        m_lgv = cur("ve")
        for tt in range(4):
            w = [("ve", m_lgv)]
            if tt == 0:
                w.append(("gp", W_CONST))
            add("te", lambda e, tt=tt: e.matmul(PS[5][:, tt * 8:(tt + 1) * 8],
                                                   lgrow[0:8, tt * 128:(tt + 1) * 128],
                                                   identf[0:8, 0:8], start=True, stop=True),
                waits=w if tt == 0 else [])
        add("te", lambda e: None, inc="te")
        m_lgt = cur("te")
        add("ve", lambda e: e.tensor_copy(lgtt[:, :], PS[5][:, 0:32]),
            waits=[("te", m_lgt), ("dsc", prev_lg_dma)], inc="ve")
        m_lgc = cur("ve")
        prev_ps5_ve = m_lgc
        for tt in range(4):
            add("sc", lambda e, i=i, tt=tt: e.dma_start(
                out=lgm[i * CHT + tt * 128:i * CHT + (tt + 1) * 128, :],
                in_=lgtt[:, tt * 8:(tt + 1) * 8]),
                waits=[("ve", m_lgc)] if tt == 0 else [], inc="dsc", n=16)
        prev_lg_dma = cur("dsc")

        m_ev = 0
        m_ev2 = 0
        for tt in range(4):
            for fc in range(FC):
                ps = PS[0] if fc < 4 else PS[1]
                w = []
                if fc == 0:
                    w = [("ve", m_h2)] if tt == 0 else [("ve", m_ev)]
                add("te", lambda e, fc=fc, tt=tt, ps=ps: e.matmul(
                    ps[:, (fc % 4) * 128:(fc % 4 + 1) * 128],
                    x1b[:, fc, tt * 128:(tt + 1) * 128], identf[:, :], start=True, stop=True), waits=w)
            add("te", lambda e: None, inc="te")
            m_tr = cur("te")
            add("ve", lambda e: e.tensor_copy(evrow[:, 0:512], PS[0][:, :]),
                waits=[("te", m_tr), ("dsc", prev_spill_dma)])
            add("ve", lambda e: e.tensor_copy(evrow[:, 512:1024], PS[1][:, :]), inc="ve")
            m_ev = cur("ve")
            add("sc", lambda e, i=i, tt=tt: e.dma_start(
                out=x1sp[i * CHT + tt * 128:i * CHT + (tt + 1) * 128, :], in_=evrow[:, :]),
                waits=[("ve", m_ev)], inc="dsc", n=16)
            prev_spill_dma = cur("dsc")
            for fc in range(FC):
                ps = PS[2] if fc < 4 else PS[3]
                w = [("ve", m_ev2)] if (fc == 0 and tt > 0) else []
                add("te", lambda e, fc=fc, tt=tt, ps=ps: e.matmul(
                    ps[:, (fc % 4) * 128:(fc % 4 + 1) * 128],
                    xnb[:, fc, tt * 128:(tt + 1) * 128], identb[:, :], start=True, stop=True), waits=w)
            add("te", lambda e: None, inc="te")
            m_tr2 = cur("te")
            add("ve", lambda e: e.tensor_copy(evrowb[:, 0:512], PS[2][:, :]),
                waits=[("te", m_tr2), ("dsc", prev_h2_dma)])
            add("ve", lambda e: e.tensor_copy(evrowb[:, 512:1024], PS[3][:, :]), inc="ve")
            m_ev2 = cur("ve")
            add("sc", lambda e, i=i, tt=tt: e.dma_start(
                out=h2m[i * CHT + tt * 128:i * CHT + (tt + 1) * 128, :], in_=evrowb[:, :]),
                waits=[("ve", m_ev2)], inc="dsc", n=16)
            prev_h2_dma = cur("dsc")

    A_DONE_VE = cur("ve")
    A_DONE_DSC = cur("dsc")

    MARK1 = {k: len(v) for k, v in SCHED.items()}
    add("gp", lambda e: e.collective_compute(
        "AllGather", ALU.bypass, replica_groups=[list(range(NCORE))],
        ins=[lgm.opt()], outs=[lgall.opt()]),
        waits=[("dsc", A_DONE_DSC)], inc="cc")
    CC_LG = cur("cc")
    add("gp", lambda e: e.collective_compute(
        "AllGather", ALU.bypass, replica_groups=[list(range(NCORE))],
        ins=[h2m.opt()], outs=[h2all.opt()]), inc="cc")
    CC_AGH = cur("cc")
    add("gp", lambda e: e.collective_compute(
        "AllReduce", ALU.add, replica_groups=[list(range(NCORE))],
        ins=[bar_in.opt()], outs=[bar_out.opt()]),
        waits=[("cc", CC_AGH)], inc="cc")
    CC_BAR1 = cur("cc")
    add("gp", lambda e: e.collective_compute(
        "AllReduce", ALU.add, replica_groups=[list(range(NCORE))],
        ins=[bar_in.opt()], outs=[bar2_out.opt()]),
        waits=[("cc", CC_BAR1)], inc="cc")
    CC_H2 = cur("cc")

    attn_st.close()
    rt_st = ExitStack()
    GL = SB("GL", [128, 256, E], F32, rt_st)
    T1 = SB("T1", [128, 256, E], F32, rt_st)
    mx = SB("mx", [128, 256], F32, rt_st)
    m1 = SB("m1", [128, 256], F32, rt_st)
    m2 = SB("m2", [128, 256], F32, rt_st)
    e1v = SB("e1v", [128, 256], F32, rt_st)
    e2v = SB("e2v", [128, 256], F32, rt_st)
    pn1 = SB("pn1", [128, 256], F32, rt_st)
    pn2 = SB("pn2", [128, 256], F32, rt_st)
    scr1 = SB("scr1", [128, 256], F32, rt_st)
    scr2 = SB("scr2", [128, 256], F32, rt_st)
    scr3 = SB("scr3", [128, 256], F32, rt_st)
    rank_all = SB("rank_all", [128, E, 256], F32, rt_st)
    exclc = SB("exclc", [128, 1], F32, rt_st)
    totb = SB("totb", [128, 1], BF16, rt_st)
    pairb = SB("pairb", [128, 256, 2], F32, rt_st)
    posi = SB("posi", [128, 256], I32, rt_st)
    s1m = SB("s1m", [128, 256], F32, rt_st)
    s2m = SB("s2m", [128, 256], F32, rt_st)
    rkme = SB("rkme", [128, 256], F32, rt_st)
    kpme = SB("kpme", [128, 256], F32, rt_st)
    wvme = SB("wvme", [128, 256], F32, rt_st)
    gk0 = SB("gk0", [128, 256], F32, rt_st)
    gk1 = SB("gk1", [128, 256], F32, rt_st)
    kk0 = SB("kk0", [128, 256], F32, rt_st)
    kk1 = SB("kk1", [128, 256], F32, rt_st)
    pk4 = SB("pk4", [128, 4, 256], F32, rt_st)
    my4 = SB("my4", [16, 4, 256], F32, rt_st)

    add("sy", lambda e: e.dma_start(out=GL[:], in_=lgall.rearrange("(p b) e -> p b e", p=128)),
        waits=[("cc", CC_BAR1)], inc="dsy", n=16)
    R_GL = cur("dsy")

    add("ve", lambda e: e.reduce_max(mx[:], GL[:], axis=AX),
        waits=[("dsy", R_GL), ("ve", A_DONE_VE)], inc="ve")
    add("ve", lambda e: None, waits=[("ve", cur("ve"))])
    for ee in range(E):
        add("ve", lambda e, ee=ee: e.tensor_tensor(T1[:, :, ee], GL[:, :, ee], mx[:], op=ALU.subtract))
    add("ve", lambda e: None, inc="ve")
    R_T1 = cur("ve")
    add("sc", lambda e: e.activation(out=GL[:], in_=T1[:], func=ACT.Exp),
        waits=[("ve", R_T1)], inc="sc")
    R_EXP = cur("sc")
    add("ve", lambda e: e.reduce_sum(mx[:], GL[:], axis=AX), waits=[("sc", R_EXP)], inc="ve")
    add("ve", lambda e: e.reciprocal(scr1[:], mx[:]), waits=[("ve", cur("ve"))])
    for ee in range(E):
        add("ve", lambda e, ee=ee: e.tensor_tensor(GL[:, :, ee], GL[:, :, ee], scr1[:], op=ALU.mult))
    add("ve", lambda e: e.reduce_max(m1[:], GL[:], axis=AX), inc="ve")
    add("ve", lambda e: e.memset(e1v[:], 8.0), waits=[("ve", cur("ve"))])
    for ee in range(E):
        add("ve", lambda e, ee=ee: e.tensor_tensor(scr1[:], GL[:, :, ee], m1[:], op=ALU.is_equal))
        add("ve", lambda e, ee=ee: e.tensor_scalar(scr1[:], scr1[:], float(ee - 8), 8.0,
                                                   op0=ALU.mult, op1=ALU.add))
        add("ve", lambda e, ee=ee: e.tensor_tensor(e1v[:], e1v[:], scr1[:], op=ALU.min))
    for ee in range(E):
        add("ve", lambda e, ee=ee: e.tensor_scalar(scr1[:], e1v[:], float(ee), None, op0=ALU.is_equal))
        add("ve", lambda e, ee=ee: e.tensor_scalar(scr1[:], scr1[:], -1.0, 1.0, op0=ALU.mult, op1=ALU.add))
        add("ve", lambda e, ee=ee: e.tensor_tensor(T1[:, :, ee], GL[:, :, ee], scr1[:], op=ALU.mult))
    add("ve", lambda e: e.reduce_max(m2[:], T1[:], axis=AX), inc="ve")
    add("ve", lambda e: e.memset(e2v[:], 8.0), waits=[("ve", cur("ve"))])
    for ee in range(E):
        add("ve", lambda e, ee=ee: e.tensor_tensor(scr1[:], T1[:, :, ee], m2[:], op=ALU.is_equal))
        add("ve", lambda e, ee=ee: e.tensor_scalar(scr1[:], scr1[:], float(ee - 8), 8.0,
                                                   op0=ALU.mult, op1=ALU.add))
        add("ve", lambda e, ee=ee: e.tensor_tensor(e2v[:], e2v[:], scr1[:], op=ALU.min))
    add("ve", lambda e: e.tensor_tensor(scr1[:], m1[:], m2[:], op=ALU.add), inc="ve")
    add("ve", lambda e: e.reciprocal(scr2[:], scr1[:]), waits=[("ve", cur("ve"))])
    add("ve", lambda e: e.tensor_tensor(pn1[:], m1[:], scr2[:], op=ALU.mult))
    add("ve", lambda e: e.tensor_tensor(pn2[:], m2[:], scr2[:], op=ALU.mult), inc="ve")
    R_TOP2 = cur("ve")

    for ee in range(E):
        add("ve", lambda e, ee=ee: e.tensor_scalar(scr1[:], e1v[:], float(ee), None, op0=ALU.is_equal))
        add("ve", lambda e, ee=ee: e.tensor_scalar(scr2[:], e2v[:], float(ee), None, op0=ALU.is_equal))
        add("ve", lambda e, ee=ee: e.tensor_tensor(scr1[:], scr1[:], scr2[:], op=ALU.add))
        add("ve", lambda e, ee=ee: e.tensor_tensor_scan(rank_all[:, ee, :], ones513[:, 0:256],
                                                        scr1[:], 0.0, op0=ALU.mult, op1=ALU.add))
        add("ve", lambda e, ee=ee: e.reduce_sum(exclc[:], scr1[:], axis=AX), inc="ve")
        m_r1 = cur("ve")
        add("ve", lambda e: e.tensor_copy(totb[:], exclc[:]), waits=[("ve", m_r1)], inc="ve")
        m_r2 = cur("ve")
        add("te", lambda e: e.matmul(PS[0][:, 0:1], l128b[:, :], totb[:, :], start=True, stop=True),
            waits=[("ve", m_r2)], inc="te")
        m_r3 = cur("te")
        add("ve", lambda e: e.tensor_copy(exclc[:], PS[0][:, 0:1]), waits=[("te", m_r3)], inc="ve")
        m_r4 = cur("ve")
        add("ve", lambda e, ee=ee: e.tensor_scalar(rank_all[:, ee, :], rank_all[:, ee, :],
                                                   exclc[:, 0:1], None, op0=ALU.add),
            waits=[("ve", m_r4)])
    add("ve", lambda e: None, inc="ve")
    R_RANK = cur("ve")

    add("ve", lambda e: e.tensor_scalar(s1m[:], e1v[:], cid_s[:, 0:1], None, op0=ALU.is_equal))
    add("ve", lambda e: e.tensor_scalar(s2m[:], e2v[:], cid_s[:, 0:1], None, op0=ALU.is_equal))
    add("ve", lambda e: e.memset(rkme[:], 0.0))
    for ee in range(E):
        add("ve", lambda e, ee=ee: e.tensor_scalar(exclc[:], cid_s[:, 0:1], float(ee), None,
                                                   op0=ALU.is_equal), inc="ve")
        m_c = cur("ve")
        add("ve", lambda e, ee=ee: e.tensor_scalar(scr1[:], rank_all[:, ee, :], exclc[:, 0:1], None,
                                                   op0=ALU.mult), waits=[("ve", m_c)])
        add("ve", lambda e: e.tensor_tensor(rkme[:], rkme[:], scr1[:], op=ALU.add))
    add("ve", lambda e: e.tensor_scalar(kpme[:], rkme[:], float(CAP), None, op0=ALU.is_le))
    add("ve", lambda e: e.tensor_tensor(scr1[:], s1m[:], pn1[:], op=ALU.mult))
    add("ve", lambda e: e.tensor_tensor(scr2[:], s2m[:], pn2[:], op=ALU.mult))
    add("ve", lambda e: e.tensor_tensor(wvme[:], scr1[:], scr2[:], op=ALU.add))
    add("ve", lambda e: e.tensor_tensor(wvme[:], wvme[:], kpme[:], op=ALU.mult))
    add("ve", lambda e: e.tensor_tensor(scr1[:], s1m[:], s2m[:], op=ALU.add))
    add("ve", lambda e: e.tensor_tensor(scr1[:], scr1[:], kpme[:], op=ALU.mult))
    add("ve", lambda e: e.tensor_scalar(scr2[:], rkme[:], -1.0, None, op0=ALU.add))
    add("ve", lambda e: e.tensor_scalar(scr3[:], scr2[:], 1.0 / 128.0, -0.49609375,
                                        op0=ALU.mult, op1=ALU.add))
    add("ve", lambda e: e.tensor_scalar(scr3[:], scr3[:], MAGIC2, -MAGIC2, op0=ALU.add, op1=ALU.add))
    add("ve", lambda e: e.scalar_tensor_tensor(scr2[:], scr3[:], -128.0, scr2[:],
                                               op0=ALU.mult, op1=ALU.add))
    add("ve", lambda e: e.scalar_tensor_tensor(scr2[:], scr2[:], 41.0, scr3[:],
                                               op0=ALU.mult, op1=ALU.add))
    add("ve", lambda e: e.tensor_scalar(scr2[:], scr2[:], float(-CPAD), None, op0=ALU.add))
    add("ve", lambda e: e.tensor_tensor(scr2[:], scr2[:], scr1[:], op=ALU.mult))
    add("ve", lambda e: e.tensor_scalar(scr2[:], scr2[:], float(CPAD), None, op0=ALU.add), inc="ve")
    add("ve", lambda e: e.tensor_copy(posi[:], scr2[:]), waits=[("ve", cur("ve"))])
    add("ve", lambda e: e.tensor_copy(pairb[:, :, 0], iotaU[:]))
    add("ve", lambda e: e.tensor_copy(pairb[:, :, 1], wvme[:]), inc="ve")
    R_PAIR = cur("ve")

    add("gp", lambda e: e.dma_start(out=listd[0:CPAD, :], in_=zerS[:, 0:82]),
        waits=[("ve", W_MEMS), ("dgp", W_W2)], inc="dgp", n=16)
    add("gp", lambda e: e.dma_start(out=listd[CPAD:CPAD + 2, :], in_=zerS[0:1, 0:4]),
        inc="dgp", n=16)
    R_LZ = cur("dgp")
    for bi in range(256):
        w = [("ve", R_PAIR), ("dgp", R_LZ)] if bi == 0 else []
        add("gp", lambda e, bi=bi: e.indirect_dma_start(
            out=listd[0:1, :], out_offset=IndirectOffsetOnAxis(posi[:, bi:bi + 1], 0),
            in_=pairb[:, bi, :], in_offset=None),
            waits=w, inc="dgp", n=16)
    R_SCAT = cur("dgp")

    add("sy", lambda e: e.dma_start(out=lsb[:], in_=listd[0:CPAD, :].rearrange("(p r) t -> p r t", p=128)),
        waits=[("dgp", R_SCAT)], inc="dsy", n=16)
    R_LSB = cur("dsy")
    add("ve", lambda e: e.tensor_copy(gidxI[:], lsb[:, :, 0]), waits=[("dsy", R_LSB)], inc="ve")
    R_GIDX = cur("ve")

    for (gk, kk, evec) in ((gk0, kk0, e1v), (gk1, kk1, e2v)):
        add("ve", lambda e, gk=gk: e.memset(gk[:], 0.0), waits=[("ve", R_RANK)])
        for ee in range(E):
            add("ve", lambda e, ee=ee, evec=evec: e.tensor_scalar(scr1[:], evec[:], float(ee), None,
                                                                  op0=ALU.is_equal))
            add("ve", lambda e, ee=ee: e.tensor_tensor(scr1[:], scr1[:], rank_all[:, ee, :], op=ALU.mult))
            add("ve", lambda e, gk=gk: e.tensor_tensor(gk[:], gk[:], scr1[:], op=ALU.add))
        add("ve", lambda e, gk=gk, kk=kk: e.tensor_scalar(kk[:], gk[:], float(CAP), None, op0=ALU.is_le))
        add("ve", lambda e, gk=gk, evec=evec: e.scalar_tensor_tensor(gk[:], evec[:], float(CPAD), gk[:],
                                                                     op0=ALU.mult, op1=ALU.add))
        add("ve", lambda e, gk=gk: e.tensor_scalar(gk[:], gk[:], -1.0, None, op0=ALU.add))
        add("ve", lambda e, gk=gk, kk=kk: e.tensor_tensor(gk[:], gk[:], kk[:], op=ALU.mult))
    add("ve", lambda e: e.tensor_copy(pk4[:, 0, :], gk0[:]))
    add("ve", lambda e: e.tensor_copy(pk4[:, 1, :], gk1[:]))
    add("ve", lambda e: e.tensor_copy(pk4[:, 2, :], kk0[:]))
    add("ve", lambda e: e.tensor_copy(pk4[:, 3, :], kk1[:]), inc="ve")
    R_PK4 = cur("ve")
    add("te", lambda e: e.matmul(PS[0][0:16, :], psel_s[:, :],
                                 pk4[:].rearrange("p a b -> p (a b)")[:, 0:512],
                                 start=True, stop=True),
        waits=[("ve", R_PK4)])
    add("te", lambda e: e.matmul(PS[1][0:16, :], psel_s[:, :],
                                 pk4[:].rearrange("p a b -> p (a b)")[:, 512:1024],
                                 start=True, stop=True), inc="te")
    R_PSEL = cur("te")
    add("ve", lambda e: e.tensor_copy(my4[0:16, 0:2, :],
                                      PS[0][0:16, :].rearrange("p (a b) -> p a b", a=2)),
        waits=[("te", R_PSEL)])
    add("ve", lambda e: e.tensor_copy(my4[0:16, 2:4, :],
                                      PS[1][0:16, :].rearrange("p (a b) -> p a b", a=2)), inc="ve")
    R_MY4 = cur("ve")
    for q in range(4):
        for bbk in range(2):
            w = [("ve", R_MY4)] if (q == 0 and bbk == 0) else []
            add("te", lambda e, q=q, bbk=bbk: e.matmul(
                PS[2][:, (q * 2 + bbk) * 16:(q * 2 + bbk + 1) * 16],
                my4[0:16, q, bbk * 128:(bbk + 1) * 128], identf[0:16, 0:16], start=True, stop=True), waits=w)
    add("te", lambda e: None, inc="te")
    R_MYT = cur("te")
    for q in range(4):
        for bbk in range(2):
            add("ve", lambda e, q=q, bbk=bbk: e.tensor_copy(
                myb[:, q, bbk, :], PS[2][:, (q * 2 + bbk) * 16:(q * 2 + bbk + 1) * 16]),
                waits=[("te", R_MYT)] if (q == 0 and bbk == 0) else [])
    add("ve", lambda e: None, inc="ve")
    add("ve", lambda e: e.tensor_copy(g0I[:], myb[:, 0, :, :]), waits=[("ve", cur("ve"))])
    add("ve", lambda e: e.tensor_copy(g1I[:], myb[:, 1, :, :]), inc="ve")
    R_BACK = cur("ve")
    MARK2 = {k: len(v) for k, v in SCHED.items()}

    rt_st.close()

    ffn_st = ExitStack()
    xgt = SB("xgt", [128, 2, D], BF16, ffn_st)
    w1s = SB("w1s", [128, 2, FC, 128], BF16, ffn_st)
    XINf = SB("XINf", [128, FC, 512], BF16, ffn_st)
    Hb = SB("Hb", [128, HC, 512], BF16, ffn_st)
    w2s = SB("w2s", [128, 8, D], BF16, ffn_st)
    t32 = SB("t32", [128, 512], F32, ffn_st)
    OUTt = SB("OUTt", [128, 2, D], BF16, ffn_st)

    m_xin_te = {}
    m_w1_cons = {}
    m_w2_cons = {}
    m_mm2_done = 0
    m_relu = {}
    m_outdma = {}
    hcg = 0

    KSETTLE = int(os.environ.get("KSETTLE", "16"))
    if KSETTLE:
        for si in range(KSETTLE):
            sw = [("cc", CC_H2)] if si == 0 else [("dgp", cur("dgp"))]
            add("gp", lambda e: e.dma_start(out=settle_d, in_=x1sp[0:128, :]),
                waits=sw, inc="dgp", n=16)

    last_buf_use = {0: None, 1: None}
    BLOCKS_ALL = BLOCKS + [(0, 4), (4, 4)]
    for b, (r0, ntk) in enumerate(BLOCKS_ALL):
        nb = ntk * 128
        m_xinf_parts = []
        for k in range(ntk):
            r = r0 + k
            w = []
            if b == 0 and r == 0:
                w = [("cc", CC_H2), ("ve", R_GIDX)]
            if last_buf_use[r % 2] is not None:
                w.append(("te", last_buf_use[r % 2]))
            if debug and b == 0 and r in (2, 3):
                w.append(("dsc", m_dump_dsc))
            if debug and b == 1 and r == 4:
                w.append(("dsc", m_dump3_dsc))
            add("gp", lambda e, r=r: e.indirect_dma_start(
                out=xgt[:, r % 2, :], out_offset=None,
                in_=h2all[0:1, :], in_offset=IndirectOffsetOnAxis(gidxI[:, r:r + 1], 0)),
                waits=w, inc="dgp", n=16)
            m_g = cur("dgp")
            if debug and b == 0 and r in (0, 1):
                add("sc", lambda e, r=r: e.dma_start(
                    out=dbg_xg[:, r, :], in_=xgt[:, r % 2, :]),
                    waits=[("dgp", m_g)], inc="dsc", n=16)
                if r == 0:
                    add("sc", lambda e: e.dma_start(out=dbg_gi, in_=gidxI[:]),
                        inc="dsc", n=16)
                m_dump_dsc = cur("dsc")
            for fc in range(FC):
                ps = PS[4] if fc < 4 else PS[5]
                w2_ = []
                if fc == 0:
                    w2_ = [("dgp", m_g)]
                    if k == 0 and b > 0:
                        w2_.append(("ve", m_mm2_done))
                    if k > 0:
                        w2_.append(("ve", m_xinf_parts[-1]))
                    if b == 0 and r == 0:
                        w2_.append(("ve", R_BACK))
                        w2_.append(("sc", prev_ps45_sc))
                add("te", lambda e, fc=fc, r=r, ps=ps: e.matmul(
                    ps[:, (fc % 4) * 128:(fc % 4 + 1) * 128],
                    xgt[:, r % 2, fc * 128:(fc + 1) * 128], identb[:, :], start=True, stop=True), waits=w2_)
            add("te", lambda e: None, inc="te")
            m_xin_te[r] = cur("te")
            last_buf_use[r % 2] = m_xin_te[r]
            add("ve", lambda e, k=k: e.tensor_copy(
                XINf[:, 0:4, k * 128:(k + 1) * 128],
                PS[4][:, :].rearrange("p (a b) -> p a b", a=4)),
                waits=[("te", m_xin_te[r])])
            add("ve", lambda e, k=k: e.tensor_copy(
                XINf[:, 4:8, k * 128:(k + 1) * 128],
                PS[5][:, :].rearrange("p (a b) -> p a b", a=4)), inc="ve")
            m_xinf_parts.append(cur("ve"))
        m_xinf = m_xinf_parts[-1]
        if debug and b == 0:
            add("sc", lambda e: e.dma_start(out=dbg_xi, in_=XINf[:, :, 0:128]),
                waits=[("ve", m_xinf)], inc="dsc", n=16)
            m_dump2_dsc = cur("dsc")

        for hc in range(HC):
            bank = PS[hc % 4]
            hcg1 = b * HC + hc
            w = [("te", m_w1_cons[hcg1 - 2])] if hcg1 >= 2 else [("dgp", W_W1B)]
            add("sy", lambda e, hc=hc, hcg1=hcg1: e.dma_start(
                out=w1s[:, hcg1 % 2, :, :], in_=w1b[hc]),
                waits=w, inc="dsy", n=16)
            m_w1l = cur("dsy")
            for fc in range(FC):
                w = []
                if fc == 0:
                    w = [("dsy", m_w1l)]
                    if hc == 0:
                        w.append(("ve", m_xinf))
                    if hc >= 4:
                        w.append(("sc", m_relu[(b, hc - 4)]))
                add("te", lambda e, hc=hc, fc=fc, bank=bank, nb=nb, hcg1=hcg1: e.matmul(
                    bank[:, 0:nb], w1s[:, hcg1 % 2, fc, :], XINf[:, fc, 0:nb],
                    start=(fc == 0), stop=(fc == FC - 1)), waits=w)
            add("te", lambda e: None, inc="te")
            m_mm1 = cur("te")
            m_w1_cons[hcg1] = m_mm1
            add("sc", lambda e, hc=hc, bank=bank, nb=nb: e.activation(
                out=Hb[:, hc, 0:nb], in_=bank[:, 0:nb], func=ACT.Relu, bias=b1_s[:, hc:hc + 1]),
                waits=[("te", m_mm1)], inc="sc")
            m_relu[(b, hc)] = cur("sc")
        if debug and b == 0:
            add("sc", lambda e: e.dma_start(out=dbg_hb, in_=Hb[:, :, 0:128]),
                waits=[("sc", m_relu[(0, HC - 1)])], inc="dsc", n=16)
            m_dump3_dsc = cur("dsc")

        for hc in range(HC):
            g2 = b * 8 + hc // 4
            if hc % 4 == 0:
                w = [("te", m_w2_cons[g2 - 2])] if g2 >= 2 else [("dgp", W_W2)]
                add("sy", lambda e, hc=hc, g2=g2: e.dma_start(
                    out=w2s[:, (g2 % 2) * 4:(g2 % 2) * 4 + 4, :].rearrange("p a b -> p (a b)"),
                    in_=w2b[:, hc:hc + 4, :].rearrange("p a b -> p (a b)")),
                    waits=w, inc="dsy", n=16)
                m_wl = cur("dsy")
            for sc_ in range(ntk):
                for dh in range(2):
                    w = []
                    if sc_ == 0 and dh == 0:
                        if hc % 4 == 0:
                            w = [("dsy", m_wl)]
                        if hc == 0:
                            w.append(("sc", m_relu[(b, HC - 1)]))
                            if b > 0:
                                w.append(("ve", m_mm2_done))
                    add("te", lambda e, hc=hc, sc_=sc_, dh=dh, g2=g2: e.matmul(
                        PS[sc_ * 2 + dh][:, :], Hb[:, hc, sc_ * 128:(sc_ + 1) * 128],
                        w2s[:, (g2 % 2) * 4 + hc % 4, dh * 512:(dh + 1) * 512],
                        start=(hc == 0), stop=(hc == HC - 1)), waits=w)
            add("te", lambda e: None, inc="te")
            if hc % 4 == 3:
                m_w2_cons[g2] = cur("te")
        m_mm2t = cur("te")

        for sc_ in range(ntk):
            r = r0 + sc_
            for dh in range(2):
                w = []
                if sc_ == 0 and dh == 0:
                    w = [("te", m_mm2t), ("dsy", R_LSB)]
                if dh == 0 and r >= 2:
                    w.append(("dsc", m_outdma.get(r - 2, 0)))
                add("ve", lambda e, sc_=sc_, dh=dh: e.tensor_tensor(
                    t32[:], PS[sc_ * 2 + dh][:, :], b2b_s[:, dh * 512:(dh + 1) * 512], op=ALU.add),
                    waits=w)
                add("ve", lambda e, sc_=sc_, dh=dh, r=r: e.tensor_scalar(
                    OUTt[:, r % 2, dh * 512:(dh + 1) * 512], t32[:], lsb[:, r, 1:2], None,
                    op0=ALU.mult))
            add("ve", lambda e: None, inc="ve")
            m_ep = cur("ve")
            add("sc", lambda e, r=r: e.dma_start(out=outm[r * 128:(r + 1) * 128, :],
                                                 in_=OUTt[:, r % 2, :]),
                waits=[("ve", m_ep)], inc="dsc", n=16)
            m_outdma[r] = cur("dsc")
        add("ve", lambda e: None, inc="ve")
        m_mm2_done = cur("ve")
    F_OUTD = cur("dsc")

    add("gp", lambda e: e.collective_compute(
        "AllGather", ALU.bypass, replica_groups=[list(range(NCORE))],
        ins=[outm.opt()], outs=[outall.opt()]),
        waits=[("dsc", F_OUTD)], inc="cc")
    CC_AGO = cur("cc")
    add("gp", lambda e: e.collective_compute(
        "AllReduce", ALU.add, replica_groups=[list(range(NCORE))],
        ins=[bar_in.opt()], outs=[bar3_out.opt()]),
        waits=[("cc", CC_AGO)], inc="cc")
    CC_BAR3 = cur("cc")
    add("gp", lambda e: e.collective_compute(
        "AllReduce", ALU.add, replica_groups=[list(range(NCORE))],
        ins=[bar_in.opt()], outs=[bar4_out.opt()]),
        waits=[("cc", CC_BAR3)], inc="cc")
    CC_OUT = cur("cc")
    MARK3 = {k: len(v) for k, v in SCHED.items()}

    ffn_st.close()

    bk_st = ExitStack()
    g0t = SB("g0t", [128, 2, D], BF16, bk_st)
    g1t = SB("g1t", [128, 2, D], BF16, bk_st)
    x1t = SB("x1t", [128, 2, D], F32, bk_st)
    acct = SB("acct", [128, D], F32, bk_st)
    ot = SB("ot", [128, 2, D], F32, bk_st)

    m_comb = {}
    m_odma = {}
    SEQ_TT = list(range(32)) + [0, 1]
    for si, tt in enumerate(SEQ_TT):
        w = [("cc", CC_OUT), ("ve", R_BACK)] if si == 0 else [("ve", m_comb[si - 2])] if si >= 2 else []
        add("gp", lambda e, tt=tt: e.indirect_dma_start(
            out=g0t[:, tt % 2, :], out_offset=None,
            in_=outall[0:1, :], in_offset=IndirectOffsetOnAxis(g0I[:, tt % 2, tt // 2:tt // 2 + 1], 0)),
            waits=w, inc="dgp", n=16)
        add("gp", lambda e, tt=tt: e.indirect_dma_start(
            out=g1t[:, tt % 2, :], out_offset=None,
            in_=outall[0:1, :], in_offset=IndirectOffsetOnAxis(g1I[:, tt % 2, tt // 2:tt // 2 + 1], 0)),
            inc="dgp", n=16)
        m_g = cur("dgp")
        w = [("dsc", A_DONE_DSC)] if si == 0 else [("ve", m_comb[si - 2])] if si >= 2 else []
        add("sy", lambda e, tt=tt: e.dma_start(out=x1t[:, tt % 2, :],
                                               in_=x1sp[tt * 128:(tt + 1) * 128, :]),
            waits=w, inc="dsy", n=16)
        m_xl = cur("dsy")
        w = [("dgp", m_g), ("dsy", m_xl)]
        if si >= 2:
            w.append(("dsc", m_odma[si - 2]))
        add("ve", lambda e, tt=tt: e.scalar_tensor_tensor(
            acct[:], g0t[:, tt % 2, :], myb[:, 2, tt % 2, tt // 2:tt // 2 + 1],
            x1t[:, tt % 2, :], op0=ALU.mult, op1=ALU.add), waits=w)
        add("ve", lambda e, tt=tt: e.scalar_tensor_tensor(
            ot[:, tt % 2, :], g1t[:, tt % 2, :], myb[:, 3, tt % 2, tt // 2:tt // 2 + 1],
            acct[:], op0=ALU.mult, op1=ALU.add), inc="ve")
        m_comb[si] = cur("ve")
        add("sc", lambda e, tt=tt: e.dma_start(out=out_t[tt * 128:(tt + 1) * 128, :],
                                               in_=ot[:, tt % 2, :]),
            waits=[("ve", m_comb[si])], inc="dsc", n=16)
        m_odma[si] = cur("dsc")
    B_OUT = cur("dsc")

    add("sc", lambda e: None, waits=[("dsc", B_OUT)])
    add("sy", lambda e: None, waits=[("dsc", B_OUT)])
    add("gp", lambda e: None, waits=[("dsc", B_OUT)])
    add("ve", lambda e: None, waits=[("dsc", B_OUT)])
    add("te", lambda e: None, waits=[("dsc", B_OUT)])

    bk_st.close()

    if limit < 4:
        mark = {1: MARK1, 2: MARK2, 3: MARK3}[limit]
        for k in SCHED:
            SCHED[k] = SCHED[k][:mark[k]]
        for s in CNT:
            CNT[s] = 0
        for k in SCHED:
            for seq, waits, inc, n, fn in SCHED[k]:
                if inc is not None:
                    CNT[inc] += n
    if os.environ.get("KSEQPRINT"):
        print("total steps:", SEQ[0])
    if limit < 4:
        tailw = [("dsc", A_DONE_DSC)]
        if limit >= 2:
            tailw.append(("ve", R_BACK))
            tailw.append(("dsy", R_LSB))
        if limit >= 3:
            tailw.append(("cc", CC_OUT))
        add("gp", lambda e: e.dma_start(out=out_t, in_=x1sp),
            waits=tailw, inc="dgp", n=16)
        TG = cur("dgp")
        add("sc", lambda e: None, waits=[("dgp", TG)])
        TEND = cur("dsc")
        add("sc", lambda e: None, waits=[("dsc", TEND)])
        add("sy", lambda e: None, waits=[("dsc", TEND)])
        add("gp", lambda e: None, waits=[("dsc", TEND)])

    if debug:
        DEND = cur("dsc")
        dbgw = [("dsc", DEND)]
        add("sc", lambda e: e.dma_start(out=dbg_x1, in_=x1sp), waits=dbgw, inc="dsc", n=16)
        add("gp", lambda e: e.dma_start(out=dbg_h2, in_=h2m), waits=dbgw, inc="dgp", n=16)
        if limit >= 2:
            add("sc", lambda e: e.dma_start(out=dbg_lg, in_=lgall), inc="dsc", n=16)
            add("sc", lambda e: e.dma_start(out=dbg_ls, in_=listd), inc="dsc", n=16)
            add("sc", lambda e: e.dma_start(out=dbg_rt[:, 0, :], in_=e1v[:]), inc="dsc", n=16)
            add("sc", lambda e: e.dma_start(out=dbg_rt[:, 1, :], in_=e2v[:]), inc="dsc", n=16)
            add("sc", lambda e: e.dma_start(out=dbg_rt[:, 2, :], in_=kpme[:]), inc="dsc", n=16)
            add("sc", lambda e: e.dma_start(out=dbg_rt[:, 3, :], in_=rkme[:]), inc="dsc", n=16)
            add("sc", lambda e: e.dma_start(out=dbg_rt[:, 4, :], in_=s1m[:]), inc="dsc", n=16)
            add("sc", lambda e: e.dma_start(out=dbg_rt[:, 5, :], in_=wvme[:]), inc="dsc", n=16)
            add("gp", lambda e: e.dma_start(out=dbg_rt[:, 6, :], in_=posi[:]), waits=dbgw, inc="dgp", n=16)
            add("sc", lambda e: e.dma_start(out=dbg_rt[:, 7, :], in_=rank_all[:, 0, :]), inc="dsc", n=16)
        if limit >= 3:
            add("gp", lambda e: e.dma_start(out=dbg_om, in_=outm), inc="dgp", n=16)
            add("gp", lambda e: e.dma_start(out=dbg_h2a, in_=h2all[0:8192, :]), inc="dgp", n=16)
        DEND2 = cur("dsc")
        DGND = cur("dgp")
        add("sc", lambda e: None, waits=[("dsc", DEND2), ("dgp", DGND)])

    sem_ctx = ExitStack()
    SEMS = {s: sem_ctx.enter_context(nc.semaphore(s)) for s in CNT}
    blk = ctx.enter_context(nc.Block())

    kcut = int(os.environ.get("KCUT", "0"))
    kept_dsc = 0
    if kcut:
        for seq, waits, inc, n, fn in SCHED["sc"]:
            if seq <= kcut and inc == "dsc":
                kept_dsc += n

    emit_cnt = {k: 0 for k in CNT}

    def mk(engname):
        def body(e):
            last_inst = None
            last_upd = None
            for seq, waits, inc, n, fn in SCHED[engname]:
                if kcut and seq > kcut:
                    continue
                for (s, v) in waits:
                    e.wait_ge(SEMS[s], v)
                inst = fn(e)
                if inst is not None:
                    last_inst = inst
                    last_upd = None
                if inc is not None:
                    emit_cnt[inc] += n
                    tgt = emit_cnt[inc]
                    if inst is not None:
                        inst.then_inc(SEMS[inc], n)
                        last_upd = (inc, tgt)
                    elif last_inst is not None and last_upd is None:
                        last_inst.then_inc(SEMS[inc], n)
                        last_upd = (inc, tgt)
                    elif last_upd is not None:
                        s0, v0 = last_upd
                        e.wait_ge(SEMS[s0], v0)
                        e.nop().then_inc(SEMS[inc], n)
                        last_upd = (inc, tgt)
                    else:
                        e.nop().then_inc(SEMS[inc], n)
            if kcut and engname == "sc":
                e.dma_start(out=out_t[0:512, :], in_=x1sp[0:512, :]).then_inc(SEMS["dsc"], 16)
                e.wait_ge(SEMS["dsc"], kept_dsc + 16)
        return body

    with sem_ctx:
        blk.sync(mk("sy"))
        blk.scalar(mk("sc"))
        blk.vector(mk("ve"))
        blk.tensor(mk("te"))
        blk.gpsimd(mk("gp"))
    ctx.close()
    return nc


_cache = {}


def kernel(L0_attn_x, L0_attn_Wk, L0_attn_Wv, L0_attn_Wo, L0_attn_freqs,
           L0_ffn_gate_w, L0_ffn_gate_b, L0_ffn_w1, L0_ffn_b1, L0_ffn_w2, L0_ffn_b2):
    x = np.asarray(L0_attn_x)
    Wv = np.asarray(L0_attn_Wv)
    Wo = np.asarray(L0_attn_Wo)
    freqs = np.asarray(L0_attn_freqs)
    gw = np.asarray(L0_ffn_gate_w)
    gb = np.asarray(L0_ffn_gate_b)
    w1 = np.asarray(L0_ffn_w1)
    b1 = np.asarray(L0_ffn_b1)
    w2 = np.asarray(L0_ffn_w2)
    b2 = np.asarray(L0_ffn_b2)

    limit = int(os.environ.get("KLIMIT", "4"))
    debug = int(os.environ.get("KDEBUG", "0"))
    key = (limit, debug)
    if key not in _cache:
        _cache[key] = build(limit, debug)
    nc = _cache[key]

    wvT = np.ascontiguousarray(Wv.T.reshape(FC, 128, D).transpose(1, 0, 2))
    woT = np.ascontiguousarray(Wo.T.reshape(FC, 128, D).transpose(1, 0, 2))
    gwl = np.ascontiguousarray(gw.reshape(FC, 128, E).transpose(1, 0, 2))
    frl = np.ascontiguousarray(freqs.reshape(FC, 128).T)
    gbl = np.ascontiguousarray(gb.reshape(E, 1))

    in_maps = []
    for c in range(NCORE):
        psel = np.zeros((128, 16), np.float32)
        psel[16 * c + np.arange(16), np.arange(16)] = 1.0
        in_maps.append({
            "xT": np.ascontiguousarray(x[c].T.reshape(FC, 128, T).transpose(1, 0, 2)),
            "wvT": wvT, "woT": woT, "gw": gwl, "gb": gbl, "fr": frl,
            "cid": np.full((128, 1), float(c), np.float32),
            "psel": psel,
            "w1": np.ascontiguousarray(w1[c]),
            "b1": np.ascontiguousarray(b1[c].reshape(HC, 128).T),
            "w2": np.ascontiguousarray(w2[c]),
            "b2": np.ascontiguousarray(b2[c].reshape(1, D)),
            "zin": np.zeros((512, D), np.float32),
        })
    res = run_bass_kernel_spmd(nc, in_maps, core_ids=list(range(NCORE)))
    out = np.stack([np.asarray(res.results[c]["yout"]) for c in range(NCORE)], axis=0)
    if debug:
        kernel._dbg = res.results
    return out
